# version 15
# baseline (speedup 1.0000x reference)
import contextlib
import numpy as np
import ml_dtypes
import concourse.bass as bass
import concourse.mybir as mybir
from concourse.bass_utils import run_bass_kernel_spmd

F32 = mybir.dt.float32
BF16 = mybir.dt.bfloat16
AF = mybir.ActivationFunctionType
ALU = mybir.AluOpType
AX = mybir.AxisListType

TWO_PI = 6.283185307179586
C1 = float(np.float32(np.ldexp(np.round(np.ldexp(TWO_PI, 11)), -11)))
C2 = float(np.float32(np.ldexp(np.round(np.ldexp(TWO_PI - C1, 33)), -33)))
C3 = float(TWO_PI - C1 - C2)
MAGIC = 12582912.0
EPS = 1e-5
D = 1024
TLOC = 2048
NT = 16
NCH = 8


class Trk:
    def __init__(self, nc, es):
        names = ["sd", "gd", "t", "v", "s", "g", "cc"]
        self.h = {n: es.enter_context(nc.semaphore(n)) for n in names}
        self.n = {n: 0 for n in names}

    def inc(self, instr, name, k=1):
        instr.then_inc(self.h[name], k)
        self.n[name] += k
        return instr

    def wait(self, eng, name, val=None):
        v = self.n[name] if val is None else val
        if v > 0:
            eng.wait_ge(self.h[name], v)


def build_nc():
    nc = bass.Bass("TRN2", target_bir_lowering=False)

    x_d = nc.dram_tensor("x", [TLOC, D], F32, kind="ExternalInput").ap()
    wvt_d = nc.dram_tensor("wvt", [D, D], F32, kind="ExternalInput").ap()
    wot_d = nc.dram_tensor("wot", [D, D], F32, kind="ExternalInput").ap()
    wft_d = nc.dram_tensor("wft", [D, D], F32, kind="ExternalInput").ap()
    f_d = nc.dram_tensor("freqs", [1, D], F32, kind="ExternalInput").ap()
    gam_d = nc.dram_tensor("gamma", [1, D], F32, kind="ExternalInput").ap()
    bet_d = nc.dram_tensor("beta", [1, D], F32, kind="ExternalInput").ap()
    idf_d = nc.dram_tensor("identf", [128, 128], F32, kind="ExternalInput").ap()
    on2_d = nc.dram_tensor("ones2d", [128, 128], F32, kind="ExternalInput").ap()
    tri_d = nc.dram_tensor("tri", [128, 128], BF16, kind="ExternalInput").ap()
    oh_d = nc.dram_tensor("onehot", [128, 16 * NT], BF16, kind="ExternalInput").ap()
    ones_d = nc.dram_tensor("onesrow", [1, 128], F32, kind="ExternalInput").ap()
    iota_d = nc.dram_tensor("iotacol", [128, 1], F32, kind="ExternalInput").ap()
    jrow_d = nc.dram_tensor("jrow", [128, 128], F32, kind="ExternalInput").ap()
    gidx_d = nc.dram_tensor("gidx", [128, 16], F32, kind="ExternalInput").ap()
    smask_d = nc.dram_tensor("smask", [128, 4], F32, kind="ExternalInput").ap()
    rmask_d = nc.dram_tensor("rmask", [128, 4], F32, kind="ExternalInput").ap()
    y_d = nc.dram_tensor("y", [TLOC, D], F32, kind="ExternalOutput").ap()
    ccm_in = [nc.dram_tensor(f"ccmi{i}", [128, 8], F32) for i in range(2)]
    ccm_out = [nc.dram_tensor(f"ccmo{i}", [128, 8], F32) for i in range(2)]
    ccc_in = nc.dram_tensor("cccin", [128, 64], F32)
    ccc_out = nc.dram_tensor("cccout", [128, 64], F32)
    tern_sp = nc.dram_tensor("ternspill", [128, NCH * D], BF16)

    x_t = x_d.rearrange("(t p) d -> t p d", p=128)
    y_t = y_d.rearrange("(t p) d -> t p d", p=128)
    wvt_c = wvt_d.rearrange("(c p) d -> c p d", p=128)
    wot_c = wot_d.rearrange("(c p) d -> c p d", p=128)
    wft_c = wft_d.rearrange("(c p) d -> c p d", p=128)

    es = contextlib.ExitStack()
    sb = lambda name, shape, dt=F32: es.enter_context(nc.sbuf_tensor(name, shape, dt))

    stream = sb("stream", [128, NT * D])
    slotA = sb("slotA", [128, NCH * D], BF16)
    slotB = sb("slotB", [128, NCH * D], BF16)
    idsf = sb("idsf", [128, 128])
    ones2 = sb("ones2", [128, 128])
    tris = sb("tris", [128, 128], BF16)
    ohs = sb("ohs", [128, 16 * NT], BF16)
    onesr = sb("onesr", [1, 128])
    iotac = sb("iotac", [128, 1])
    gidxs = sb("gidxs", [128, 16])
    smasks = sb("smasks", [128, 4])
    rmasks = sb("rmasks", [128, 4])
    gcols = sb("gcols", [128, NCH])
    bcols = sb("bcols", [128, NCH])
    fcols = sb("fcols", [128, NCH])
    sums = sb("sums", [128, NT])
    sumsq = sb("sumsq", [128, NT])
    negmu = sb("negmu", [128, NT])
    rho = sb("rho", [128, NT])
    rw0 = sb("rw0", [128, NT])
    rw1 = sb("rw1", [128, NT])
    rw2 = sb("rw2", [128, NT])
    maxc = sb("maxc", [128, 1])
    maxc2 = sb("maxc2", [128, 1])
    maxcols = sb("maxcols", [128, 8])
    srow = sb("srow", [1, 128])
    gcol = sb("gcol", [128, 1])
    sc = sb("sc", [1, 16])
    invs_col = sb("invs_col", [128, 1])
    sa_col = sb("sa_col", [128, 1])
    dcol = sb("dcol", [128, 1])
    ccmax = sb("ccmax", [1, 8])
    scol = sb("scol", [128, 8])
    acol = sb("acol", [128, 1])
    mcol = sb("mcol", [128, 1])
    angk = sb("angk", [128, D])
    vshared = sb("vshared", [128, D])

    T = Trk(nc, es)
    block = es.enter_context(nc.Block())


    def vflush(e):
        T.inc(e.drain(), "v")

    def sflush(e):
        T.inc(e.drain(), "s")

    def vspace(e, n=8):
        T.inc(e.drain(), "v")

    def emit_rr(e, dst, src, scr, short=False):
        dr = (lambda: e.drain()) if short else (lambda: None)
        T.inc(e.tensor_scalar(scr, src, float(1.0 / TWO_PI), float(MAGIC), ALU.mult, ALU.add), "v")
        dr()
        T.inc(e.tensor_scalar(scr, scr, float(-MAGIC), None, ALU.add), "v")
        dr()
        T.inc(e.scalar_tensor_tensor(dst, scr, float(-C1), src, ALU.mult, ALU.add), "v")
        dr()
        T.inc(e.scalar_tensor_tensor(dst, scr, float(-C2), dst, ALU.mult, ALU.add), "v")
        dr()
        T.inc(e.scalar_tensor_tensor(dst, scr, float(-C3), dst, ALU.mult, ALU.add), "v")

    def emit_wrap(e, dst, src, scr, short=False):
        dr = (lambda: e.drain()) if short else (lambda: None)
        T.inc(e.tensor_scalar(dst, src, float(np.pi / 2), None, ALU.add), "v")
        dr()
        T.inc(e.tensor_scalar(scr, dst, float(np.pi), None, ALU.is_gt), "v")
        dr()
        T.inc(e.scalar_tensor_tensor(dst, scr, float(-TWO_PI), dst, ALU.mult, ALU.add), "v")

    wstg = [sb(f"wstg{i}", [128, D]) for i in range(2)]
    ph0 = contextlib.ExitStack()
    stage = ph0.enter_context(nc.sbuf_tensor("stage", [128, NCH * D], F32))
    p0 = ph0.enter_context(nc.psum_tensor("p0", [128, 512], F32))

    @block.sync
    def _(e):
        for tt in range(NT):
            T.inc(e.dma_start(out=stream[:, tt * D:(tt + 1) * D], in_=x_t[tt]), "sd", 16)
        for c in range(NCH):
            T.inc(e.dma_start(out=stage[:, c * D:(c + 1) * D], in_=wft_c[c]), "sd", 16)
        for src, dst in [(idf_d, idsf), (on2_d, ones2), (tri_d, tris), (oh_d, ohs), (ones_d, onesr),
                         (iota_d, iotac), (gidx_d, gidxs),
                         (smask_d, smasks), (rmask_d, rmasks)]:
            T.inc(e.dma_start(out=dst[:], in_=src), "sd", 16)
        T.inc(e.dma_start(out=gcols[:], in_=gam_d.rearrange("r (c p) -> p (r c)", p=128), allow_slow_non_contiguous=True), "sd", 16)
        T.inc(e.dma_start(out=bcols[:], in_=bet_d.rearrange("r (c p) -> p (r c)", p=128), allow_slow_non_contiguous=True), "sd", 16)
        T.inc(e.dma_start(out=fcols[:], in_=f_d.rearrange("r (c p) -> p (r c)", p=128), allow_slow_non_contiguous=True), "sd", 16)
    sd_loads = T.n["sd"]

    @block.scalar
    def _(e):
        T.wait(e, "sd", sd_loads)
        for c in range(NCH):
            T.inc(e.activation(slotA[:, c * D:(c + 1) * D], stage[:, c * D:(c + 1) * D],
                               AF.Sign), "s")
        sflush(e)
    s0 = T.n["s"]

    @block.vector
    def _(e):
        T.wait(e, "sd", sd_loads)
        T.inc(e.memset(mcol[:], MAGIC), "v")
        T.inc(e.tensor_reduce(maxc[:], stage[:], AX.X, ALU.add,
                              apply_absolute_value=True), "v")
        vflush(e)
    v1 = T.n["v"]

    @block.tensor
    def _(e):
        T.wait(e, "v", v1)
        T.inc(e.matmul(p0[:, 0:1], ones2[:], maxc[:], start=True, stop=True), "t")
    t1 = T.n["t"]

    @block.vector
    def _(e):
        T.wait(e, "t", t1)
        T.inc(e.tensor_scalar(dcol[:], p0[:, 0:1], float(-0.7 / (D * D)), None, ALU.mult), "v")
        vflush(e)
    v_dcol = T.n["v"]

    ph0.close()

    state = {"tern_v": [], "mk_s": [], "mk_gd": []}

    def hook_grp(grp):
        c = grp

        @block.sync
        def _(e, c=c):
            if c >= 2:
                T.wait(e, "s", state["mk_s"][c - 2])
            T.inc(e.dma_start(out=wstg[c % 2][:], in_=wft_c[c]), "sd", 16)
        state["mk_gd"].append(T.n["sd"])

        @block.scalar
        def _(e, c=c):
            T.wait(e, "sd", state["mk_gd"][c])
            if c == 0:
                T.wait(e, "v", v_dcol)
            else:
                T.wait(e, "v", state["tern_v"][-1])
            T.inc(e.activation(angk[:], wstg[c % 2][:], AF.Abs), "s")
            e.drain()
            T.inc(e.activation(vshared[:], angk[:], AF.Sign,
                               bias=dcol[:], scale=1.0), "s")
            sflush(e)
        state["mk_s"].append(T.n["s"])

        @block.vector
        def _(e, c=c, s_m=state["mk_s"][-1]):
            T.wait(e, "s", s_m)
            T.inc(e.tensor_scalar(vshared[:], vshared[:], 0.5, 0.5, ALU.mult, ALU.add), "v")
            T.inc(e.tensor_tensor(slotA[:, c * D:(c + 1) * D],
                                  slotA[:, c * D:(c + 1) * D], vshared[:], ALU.mult), "v")
            vflush(e)
        state["tern_v"].append(T.n["v"])

    def hook_gm(pc):
        v_tern = state["tern_v"][-1]
        acc_gd = []
        acc_s = []
        for c in range(NCH):
            @block.sync
            def _(e, c=c):
                if c < 2:
                    T.wait(e, "s", state["mk_s"][-1])
                else:
                    T.wait(e, "s", acc_s[c - 2])
                T.inc(e.dma_start(out=wstg[c % 2][:], in_=wft_c[c]), "sd", 16)
            acc_gd.append(T.n["sd"])

            @block.scalar
            def _(e, c=c):
                T.wait(e, "sd", acc_gd[c])
                if c == 0:
                    T.wait(e, "v", v_tern)
                T.inc(e.activation(angk[:], wstg[c % 2][:], AF.Abs), "s")
                e.drain()
                T.inc(e.activation(vshared[:], angk[:], AF.Relu,
                                   bias=dcol[:], scale=1.0,
                                   accum_out=rw1[:, c:c + 1]), "s")
                T.inc(e.activation(vshared[:], angk[:], AF.Sign,
                                   bias=dcol[:], scale=1.0,
                                   accum_out=rw1[:, 8 + c:9 + c]), "s")
                sflush(e)
            acc_s.append(T.n["s"])
        s1 = acc_s[-1]
        state["s_acc"] = s1

        @block.vector
        def _(e, s1=s1):
            T.wait(e, "s", s1)
            T.inc(e.tensor_reduce(rw0[:, 0:1], rw1[:, 0:8], AX.X, ALU.add), "v")
            T.inc(e.tensor_reduce(rw0[:, 1:2], rw1[:, 8:16], AX.X, ALU.add), "v")
            vflush(e)
        v_red2 = T.n["v"]

        @block.tensor
        def _(e, v_red2=v_red2):
            T.wait(e, "v", v_red2)
            T.inc(e.matmul(pc[:, 256:257], ones2[:], rw0[:, 0:1], start=True, stop=True), "t")
            T.inc(e.matmul(pc[:, 257:258], ones2[:], rw0[:, 1:2], start=True, stop=True), "t")
        t_bc2 = T.n["t"]

        @block.vector
        def _(e, t_bc2=t_bc2):
            T.wait(e, "t", t_bc2)
            T.inc(e.tensor_copy(scol[:, 0:1], pc[:, 256:257]), "v")
            T.inc(e.tensor_scalar(scol[:, 2:3], pc[:, 257:258], float(D * D), 0.5,
                                  ALU.add, ALU.mult), "v")
            e.drain()
            T.inc(e.tensor_scalar(scol[:, 2:3], scol[:, 2:3], 1.0, None, ALU.max), "v")
            vflush(e)
        v_cnt = T.n["v"]

        @block.scalar
        def _(e, v_cnt=v_cnt):
            T.wait(e, "v", v_cnt)
            T.inc(e.activation(scol[:, 3:4], scol[:, 2:3], AF.Ln), "s")
            e.drain()
            T.inc(e.activation(scol[:, 4:5], scol[:, 3:4], AF.Exp, scale=np.float32(-1.0)), "s")
            sflush(e)
        s_rcnt = T.n["s"]

        @block.vector
        def _(e, s_rcnt=s_rcnt):
            T.wait(e, "s", s_rcnt)
            T.inc(e.tensor_tensor(scol[:, 5:6], scol[:, 2:3], scol[:, 4:5], ALU.mult), "v")
            e.drain()
            T.inc(e.tensor_scalar(scol[:, 5:6], scol[:, 5:6], -1.0, 2.0, ALU.mult, ALU.add), "v")
            e.drain()
            T.inc(e.tensor_tensor(scol[:, 4:5], scol[:, 4:5], scol[:, 5:6], ALU.mult), "v")
            e.drain()
            T.inc(e.tensor_tensor(scol[:, 5:6], dcol[:], scol[:, 2:3], ALU.mult), "v")
            e.drain()
            T.inc(e.tensor_tensor(scol[:, 5:6], scol[:, 0:1], scol[:, 5:6], ALU.subtract), "v")
            e.drain()
            T.inc(e.tensor_tensor(acol[:], scol[:, 5:6], scol[:, 4:5], ALU.mult), "v")
            vflush(e)

        @block.sync
        def _(e):
            T.wait(e, "v", v_tern)
            T.inc(e.dma_start(out=tern_sp.ap(), in_=slotA[:]), "sd", 16)
        state["sd_spill"] = T.n["sd"]

    def emit_ffn(idx, tern, stream_ready, tern_ready=None, evac_s=True, hooks=None):
        ph = contextlib.ExitStack()
        xnF = ph.enter_context(nc.sbuf_tensor(f"xnF{idx}", [128, NCH * TLOC], F32))
        z0a = [ph.enter_context(nc.sbuf_tensor(f"z0a{idx}{i}", [128, D], F32)) for i in range(2)]
        z0b = [ph.enter_context(nc.sbuf_tensor(f"z0b{idx}{i}", [128, D], F32)) for i in range(2)]
        intg = [ph.enter_context(nc.sbuf_tensor(f"intg{idx}{i}", [128, NCH * 256], BF16))
                for i in range(2)]
        rmg = ph.enter_context(nc.sbuf_tensor(f"rmg{idx}", [128, 256], F32))
        ps_tap = contextlib.ExitStack()
        zfw = [ps_tap.enter_context(nc.psum_tensor(f"zfw{idx}{i}", [128, 2048], F32))
               for i in range(2)]

        sem, cnt = stream_ready

        @block.scalar
        def _(e):
            T.wait(e, sem, cnt)
            for tt in range(NT):
                T.inc(e.activation(angk[:], stream[:, tt * D:(tt + 1) * D], AF.Square,
                                   accum_out=sumsq[:, tt:tt + 1]), "s")
            sflush(e)
        s_st = T.n["s"]

        @block.vector
        def _(e):
            T.wait(e, sem, cnt)
            for tt in range(NT):
                T.inc(e.tensor_scalar(vshared[:], stream[:, tt * D:(tt + 1) * D],
                                      1.0, 0.0, ALU.mult, ALU.add,
                                      accum_out=sums[:, tt:tt + 1]), "v")
            T.wait(e, "s", s_st)
            e.drain()
            T.inc(e.tensor_scalar(negmu[:], sums[:], float(-1.0 / D), None, ALU.mult), "v")
            T.inc(e.tensor_scalar(rw0[:], sumsq[:], float(1.0 / D), None, ALU.mult), "v")
            e.drain()
            T.inc(e.tensor_tensor(rw2[:], negmu[:], negmu[:], ALU.mult), "v")
            e.drain()
            T.inc(e.tensor_tensor(rw0[:], rw0[:], rw2[:], ALU.subtract), "v")
            e.drain()
            T.inc(e.tensor_scalar(rw1[:], rw0[:], float(1.0 + EPS), float(EPS * EPS),
                                  ALU.mult, ALU.add), "v")
            vflush(e)
        v_st = T.n["v"]

        @block.scalar
        def _(e):
            T.wait(e, "v", v_st)
            T.inc(e.activation(rw2[:], rw1[:], AF.Ln), "s")
            e.drain()
            T.inc(e.activation(rho[:], rw2[:], AF.Exp, scale=np.float32(-0.5)), "s")
            sflush(e)
        s_rho = T.n["s"]

        @block.vector
        def _(e):
            T.wait(e, "s", s_rho)
            T.inc(e.tensor_tensor(rw2[:], rho[:], rho[:], ALU.mult), "v")
            e.drain()
            T.inc(e.tensor_tensor(rw2[:], rw1[:], rw2[:], ALU.mult), "v")
            e.drain()
            T.inc(e.tensor_scalar(rw2[:], rw2[:], -0.5, 1.5, ALU.mult, ALU.add), "v")
            e.drain()
            T.inc(e.tensor_tensor(rho[:], rho[:], rw2[:], ALU.mult), "v")
            vflush(e)
        v_rho = T.n["v"]

        ev_sem = "s" if evac_s else "v"
        z_v = []
        tap_t = []
        ev_s = []

        def emit_zprep(grp):
            ta, tb = 2 * grp, 2 * grp + 1

            @block.vector
            def _(e, ta=ta, tb=tb, grp=grp, za_=z0a[grp % 2], zb_=z0b[grp % 2]):
                if grp == 0:
                    T.wait(e, "v", v_rho)
                if grp >= 2:
                    T.wait(e, "t", tap_t[grp - 2])
                T.inc(e.tensor_scalar(za_[:], stream[:, ta * D:(ta + 1) * D],
                                      negmu[:, ta:ta + 1], rho[:, ta:ta + 1],
                                      ALU.add, ALU.mult), "v")
                T.inc(e.tensor_scalar(zb_[:], stream[:, tb * D:(tb + 1) * D],
                                      negmu[:, tb:tb + 1], rho[:, tb:tb + 1],
                                      ALU.add, ALU.mult), "v")
                vflush(e)
            z_v.append(T.n["v"])

        emit_zprep(0)
        for grp in range(NT // 2):
            if grp + 1 < NT // 2:
                emit_zprep(grp + 1)

            @block.tensor
            def _(e, grp=grp, za_=z0a[grp % 2], zb_=z0b[grp % 2], zf_=zfw[grp % 2]):
                T.wait(e, "v", z_v[grp])
                if grp >= 2:
                    T.wait(e, ev_sem, ev_s[grp - 2])
                for c in range(NCH):
                    T.inc(e.matmul(zf_[:, c * 256: c * 256 + 128],
                                   za_[:, c * 128:(c + 1) * 128],
                                   idsf[:], start=True, stop=True), "t")
                    T.inc(e.matmul(zf_[:, c * 256 + 128: c * 256 + 256],
                                   zb_[:, c * 128:(c + 1) * 128],
                                   idsf[:], start=True, stop=True), "t")
            tap_t.append(T.n["t"])

            if evac_s:
                @block.scalar
                def _(e, grp=grp, t_tap=tap_t[-1], zf_=zfw[grp % 2]):
                    T.wait(e, "t", t_tap)
                    for c in range(NCH):
                        T.inc(e.activation(
                            xnF[:, c * TLOC + grp * 256: c * TLOC + (grp + 1) * 256],
                            zf_[:, c * 256:(c + 1) * 256], AF.Identity,
                            scale=gcols[:, c:c + 1], bias=bcols[:, c:c + 1]), "s")
                    sflush(e)
                ev_s.append(T.n["s"])
            else:
                @block.vector
                def _(e, grp=grp, t_tap=tap_t[-1], zf_=zfw[grp % 2]):
                    T.wait(e, "t", t_tap)
                    for c in range(NCH):
                        T.inc(e.tensor_scalar(
                            xnF[:, c * TLOC + grp * 256: c * TLOC + (grp + 1) * 256],
                            zf_[:, c * 256:(c + 1) * 256], gcols[:, c:c + 1],
                            bcols[:, c:c + 1], ALU.mult, ALU.add), "v")
                    vflush(e)
                ev_s.append(T.n["v"])
            if hooks:
                hooks["grp"](grp)
        s_xnf = ev_s[-1]
        ps_tap.close()
        ps_gm = contextlib.ExitStack()
        pc = ps_gm.enter_context(nc.psum_tensor(f"pc{idx}", [128, 512], F32))
        pr = ps_gm.enter_context(nc.psum_tensor(f"pr{idx}", [1, 512], F32))

        @block.vector
        def _(e):
            T.wait(e, ev_sem, s_xnf)
            for c in range(NCH):
                T.inc(e.tensor_reduce(maxcols[:, c:c + 1], xnF[:, c * TLOC:(c + 1) * TLOC],
                                      AX.X, ALU.max, apply_absolute_value=True), "v")
            vflush(e)
        v_mx = T.n["v"]

        @block.gpsimd
        def _(e):
            T.wait(e, "v", v_mx)
            T.inc(e.dma_start(out=ccm_in[idx].ap(), in_=maxcols[:]), "gd", 16)
            T.wait(e, "gd")
            T.inc(e.collective_compute("AllReduce", ALU.max,
                                       replica_groups=[list(range(8))],
                                       ins=[ccm_in[idx].ap().opt()],
                                       outs=[ccm_out[idx].ap().opt()]), "cc")
            T.wait(e, "cc")
            T.inc(e.dma_start(out=maxcols[:], in_=ccm_out[idx].ap()), "gd", 16)
        gd_mx = T.n["gd"]
        if hooks:
            hooks["gm"](pc)

        @block.vector
        def _(e):
            T.wait(e, "gd", gd_mx)
            T.inc(e.tensor_reduce(maxc[:], maxcols[:], AX.X, ALU.max), "v")
            vflush(e)
        v_mx2 = T.n["v"]

        @block.tensor
        def _(e):
            T.wait(e, "v", v_mx2)
            T.inc(e.matmul(pr[0:1, 0:128], maxc[:], idsf[:], start=True, stop=True), "t")
        t_gm1 = T.n["t"]

        @block.scalar
        def _(e):
            T.wait(e, "t", t_gm1)
            T.inc(e.activation(srow[:], pr[0:1, 0:128], AF.Identity), "s")
            sflush(e)
        s_gm1 = T.n["s"]

        @block.tensor
        def _(e):
            T.wait(e, "s", s_gm1)
            T.inc(e.matmul(pc[:, 0:128], onesr[:], srow[:], start=True, stop=True), "t")
        t_gm2 = T.n["t"]

        @block.vector
        def _(e):
            T.wait(e, "t", t_gm2)
            T.inc(e.tensor_reduce(gcol[:], pc[:, 0:128], AX.X, ALU.max), "v")
            e.drain()
            T.inc(e.tensor_scalar(scol[:, 6:7], gcol[:], 1e-8, None, ALU.max), "v")
            vflush(e)
        v_gm = T.n["v"]

        @block.scalar
        def _(e):
            T.wait(e, "v", v_gm)
            T.inc(e.activation(scol[:, 7:8], scol[:, 6:7], AF.Ln), "s")
            e.drain()
            T.inc(e.activation(invs_col[:], scol[:, 7:8], AF.Exp, scale=np.float32(-1.0)), "s")
            sflush(e)
        s_gm = T.n["s"]

        @block.vector
        def _(e):
            T.wait(e, "s", s_gm)
            T.inc(e.tensor_tensor(scol[:, 7:8], scol[:, 6:7], invs_col[:], ALU.mult), "v")
            e.drain()
            T.inc(e.tensor_scalar(scol[:, 7:8], scol[:, 7:8], -1.0, 2.0, ALU.mult, ALU.add), "v")
            e.drain()
            T.inc(e.tensor_tensor(invs_col[:], invs_col[:], scol[:, 7:8], ALU.mult), "v")
            e.drain()
            T.inc(e.tensor_scalar(invs_col[:], invs_col[:], 127.0, None, ALU.mult), "v")
            T.inc(e.tensor_tensor(sa_col[:], scol[:, 6:7], acol[:], ALU.mult), "v")
            e.drain()
            T.inc(e.tensor_scalar(sa_col[:], sa_col[:], float(0.5 / 127.0), None, ALU.mult), "v")
            vflush(e)
        v_cols = T.n["v"]
        ps_gm.close()
        ps_mm = contextlib.ExitStack()
        fp2 = [[ps_mm.enter_context(nc.psum_tensor(f"fp{idx}{i}{j}", [128, 512], F32))
                for j in range(4)] for i in range(2)]

        mm_t = []
        ep_q = []

        def emit_epi(tg):
            @block.vector
            def _(e, tg=tg, t_mm=mm_t[tg]):
                T.wait(e, "t", t_mm)
                fp = fp2[tg % 2]
                for tp in range(2):
                    tt = tg * 2 + tp
                    for half in range(2):
                        sl = stream[:, tt * D + half * 512: tt * D + (half + 1) * 512]
                        T.inc(e.scalar_tensor_tensor(sl, fp[tp * 2 + half][:], sa_col[:],
                                                     sl, ALU.mult, ALU.add), "v")
                vflush(e)
            ep_q.append(T.n["v"])

        for tg in range(NT // 2):
            ig = intg[tg % 2]

            @block.vector
            def _(e, tg=tg):
                if tg == 0:
                    T.wait(e, "v", v_cols)
                if len(mm_t) >= 2:
                    T.wait(e, "t", mm_t[-2])
                for c in range(NCH):
                    T.inc(e.tensor_scalar(rmg[:],
                                          xnF[:, c * TLOC + tg * 256: c * TLOC + (tg + 1) * 256],
                                          invs_col[:], mcol[:], ALU.mult, ALU.add), "v")
                    e.drain()
                    T.inc(e.tensor_scalar(ig[:, c * 256:(c + 1) * 256], rmg[:],
                                          float(-MAGIC), None, ALU.add), "v")
                vflush(e)
            v_int = T.n["v"]

            @block.tensor
            def _(e, tg=tg, v_int=v_int, ig=ig):
                T.wait(e, "v", v_int)
                if tg == 0 and tern_ready is not None:
                    tr = tern_ready() if callable(tern_ready) else tern_ready
                    T.wait(e, tr[0], tr[1])
                if tg >= 2:
                    T.wait(e, "v", ep_q[tg - 2])
                for c in range(NCH):
                    for tp in range(2):
                        for half in range(2):
                            T.inc(e.matmul(fp2[tg % 2][tp * 2 + half][:],
                                           ig[:, c * 256 + tp * 128: c * 256 + tp * 128 + 128],
                                           tern[:, c * D + half * 512: c * D + (half + 1) * 512],
                                           start=(c == 0), stop=(c == NCH - 1),
                                           skip_group_check=True), "t")
            mm_t.append(T.n["t"])
            if tg >= 1:
                emit_epi(tg - 1)
        emit_epi(NT // 2 - 1)
        v_done = T.n["v"]
        t_done = T.n["t"]
        ps_mm.close()
        ph.close()
        return v_done, t_done

    v_ffn1, t_ffn1 = emit_ffn(0, slotA, ("sd", sd_loads),
                              tern_ready=lambda: ("v", state["tern_v"][-1]),
                              evac_s=False, hooks={"grp": hook_grp, "gm": hook_gm})
    sd_spill = state["sd_spill"]

    ph = contextlib.ExitStack()
    cosB = ph.enter_context(nc.sbuf_tensor("cosB", [128, D], BF16))
    sinB = ph.enter_context(nc.sbuf_tensor("sinB", [128, D], BF16))
    cosT = ph.enter_context(nc.sbuf_tensor("cosT", [128, D], BF16))
    sinT = ph.enter_context(nc.sbuf_tensor("sinT", [128, D], BF16))
    ugc = ph.enter_context(nc.sbuf_tensor("ugc", [128, 128], F32))
    ugs = ph.enter_context(nc.sbuf_tensor("ugs", [128, 128], F32))
    u16c = ph.enter_context(nc.sbuf_tensor("u16c", [128, 8], F32))
    u16s = ph.enter_context(nc.sbuf_tensor("u16s", [128, 8], F32))
    p_all = ph.enter_context(nc.sbuf_tensor("p_all", [128, NT * D], BF16))
    q_all = ph.enter_context(nc.sbuf_tensor("q_all", [128, NT * D], BF16))
    z3 = [ph.enter_context(nc.sbuf_tensor(f"z3{i}", [128, D], BF16)) for i in range(2)]
    ids16 = ph.enter_context(nc.sbuf_tensor("ids16", [128, 128], BF16))
    xn2g = [ph.enter_context(nc.sbuf_tensor(f"xn2g{i}", [128, NCH * 128], BF16)) for i in range(2)]
    retf = ph.enter_context(nc.sbuf_tensor("retf", [128, D], BF16))
    rs1 = ph.enter_context(nc.sbuf_tensor("rs1", [128, 512], F32))
    care = rs1[:, 0:128]
    caim = rs1[:, 128:256]
    sca = rs1[:, 256:384]
    scb = rs1[:, 384:512]
    carw = ph.enter_context(nc.sbuf_tensor("carw", [128, D], F32))
    totFp = ph.enter_context(nc.sbuf_tensor("totFp", [128, 128], F32))
    totFq = ph.enter_context(nc.sbuf_tensor("totFq", [128, 128], F32))
    w15 = ph.enter_context(nc.sbuf_tensor("w15", [128, 16], F32))
    sendb = ph.enter_context(nc.sbuf_tensor("sendb", [128, 16], F32))
    msend = ph.enter_context(nc.sbuf_tensor("msend", [128, 64], F32))
    rbuf = ph.enter_context(nc.sbuf_tensor("rbuf", [128, 64], F32))
    c0buf = ph.enter_context(nc.sbuf_tensor("c0buf", [128, 16], F32))

    @block.gpsimd
    def _(e):
        T.wait(e, "sd", sd_loads)
        T.inc(e.dma_start(out=ids16[:], in_=idf_d), "gd", 16)
        for c in range(NCH):
            T.inc(e.dma_start(out=slotB[:, c * D:(c + 1) * D], in_=wvt_c[c]), "gd", 16)
    gd_wvt = T.n["gd"]

    jrows = wstg[0][:, 0:128]
    f_late = wstg[1][0:1, 0:1024]

    @block.gpsimd
    def _(e):
        T.wait(e, "sd", sd_spill)
        T.wait(e, "s", state["s_acc"])
        T.wait(e, "t", t_ffn1)
        T.inc(e.dma_start(out=jrows, in_=jrow_d), "gd", 16)
        T.inc(e.dma_start(out=f_late, in_=f_d), "gd", 16)
        gd_jr_ = T.n["gd"]
        for c in range(NCH):
            T.inc(e.dma_start(out=slotA[:, c * D:(c + 1) * D], in_=wot_c[c]), "gd", 16)
        state["gd_jr"] = gd_jr_
    gd_wot = T.n["gd"]

    ptb = contextlib.ExitStack()
    pB1 = ptb.enter_context(nc.psum_tensor("pB1", [128, 512], F32))

    for hc in range(2):
        @block.tensor
        def _(e, hc=hc):
            if hc == 0:
                T.wait(e, "v", v_ffn1)
                T.wait(e, "gd", state["gd_jr"])
            else:
                T.wait(e, "v", T.n["v"])
            for q in range(4):
                T.inc(e.matmul(pB1[:, q * 128:(q + 1) * 128], onesr[:],
                               f_late[0:1, hc * 512 + q * 128: hc * 512 + (q + 1) * 128],
                               start=True, stop=True), "t")

        @block.vector
        def _(e, hc=hc):
            T.wait(e, "t", T.n["t"])
            if hc == 0:
                T.wait(e, "s", state["s_acc"])
            sl = slice(hc * 512, (hc + 1) * 512)
            T.inc(e.tensor_scalar(vshared[:, sl], pB1[:, 0:512], iotac[:], None, ALU.mult), "v")
            emit_rr(e, vshared[:, sl], vshared[:, sl], angk[:, sl])
            emit_wrap(e, angk[:, sl], vshared[:, sl], carw[:, sl])
            vflush(e)
    v_bth = T.n["v"]

    @block.scalar
    def _(e):
        T.wait(e, "v", v_bth)
        T.inc(e.activation(sinB[:], vshared[:], AF.Sin), "s")
        T.inc(e.activation(cosB[:], angk[:], AF.Sin), "s")
        sflush(e)
    s_btab = T.n["s"]

    @block.vector
    def _(e):
        T.wait(e, "s", s_btab)
        T.wait(e, "gd", state["gd_jr"])
        for c in range(NCH):
            T.inc(e.tensor_scalar(vshared[:, c * 128:(c + 1) * 128], jrows,
                                  fcols[:, c:c + 1], None, ALU.mult), "v")
        emit_rr(e, vshared[:], vshared[:], angk[:])
        emit_wrap(e, angk[:], vshared[:], carw[:])
        vflush(e)
    v_tth = T.n["v"]

    @block.scalar
    def _(e):
        T.wait(e, "v", v_tth)
        T.inc(e.activation(sinT[:], vshared[:], AF.Sin), "s")
        T.inc(e.activation(cosT[:], angk[:], AF.Sin), "s")
        sflush(e)
    s_ttab = T.n["s"]

    @block.vector
    def _(e):
        T.wait(e, "s", s_ttab)
        T.inc(e.tensor_scalar(vshared[:, 0:8], fcols[:], 128.0, None, ALU.mult), "v")
        e.drain()
        emit_rr(e, vshared[:, 0:8], vshared[:, 0:8], angk[:, 0:8], short=True)
        e.drain()
        for c in range(NCH):
            T.inc(e.tensor_scalar(carw[:, c * 16:(c + 1) * 16], gidxs[:],
                                  vshared[:, c:c + 1], None, ALU.mult), "v")
        e.drain()
        emit_rr(e, carw[:, 0:128], carw[:, 0:128], angk[:, 0:128], short=True)
        T.inc(e.tensor_scalar(vshared[:, 8:16], vshared[:, 0:8], 16.0, None, ALU.mult), "v")
        e.drain()
        emit_rr(e, vshared[:, 8:16], vshared[:, 8:16], angk[:, 0:8], short=True)
        e.drain()
        emit_wrap(e, angk[:, 0:128], carw[:, 0:128], angk[:, 128:256], short=True)
        e.drain()
        emit_wrap(e, vshared[:, 16:24], vshared[:, 8:16], angk[:, 256:264], short=True)
        vflush(e)
    v_uth = T.n["v"]

    @block.scalar
    def _(e):
        T.wait(e, "v", v_uth)
        T.inc(e.activation(ugs[:], carw[:, 0:128], AF.Sin), "s")
        T.inc(e.activation(ugc[:], angk[:, 0:128], AF.Sin), "s")
        T.inc(e.activation(u16s[:], vshared[:, 8:16], AF.Sin), "s")
        T.inc(e.activation(u16c[:], vshared[:, 16:24], AF.Sin), "s")
        sflush(e)
    s_utab = T.n["s"]

    @block.scalar
    def _(e):
        T.wait(e, "s", s_utab)
        for tt in range(NT):
            T.inc(e.activation(angk[:], stream[:, tt * D:(tt + 1) * D], AF.Square,
                               accum_out=sumsq[:, tt:tt + 1]), "s")
        sflush(e)
    s_st3 = T.n["s"]

    @block.vector
    def _(e):
        T.wait(e, "s", s_utab)
        for tt in range(NT):
            T.inc(e.tensor_scalar(vshared[:], stream[:, tt * D:(tt + 1) * D],
                                  1.0, 0.0, ALU.mult, ALU.add,
                                  accum_out=sums[:, tt:tt + 1]), "v")
        T.wait(e, "s", s_st3)
        e.drain()
        T.inc(e.tensor_scalar(negmu[:], sums[:], float(-1.0 / D), None, ALU.mult), "v")
        T.inc(e.tensor_scalar(rw0[:], sumsq[:], float(1.0 / D), None, ALU.mult), "v")
        e.drain()
        T.inc(e.tensor_tensor(rw1[:], negmu[:], negmu[:], ALU.mult), "v")
        e.drain()
        T.inc(e.tensor_tensor(rw0[:], rw0[:], rw1[:], ALU.subtract), "v")
        e.drain()
        T.inc(e.tensor_scalar(rw1[:], rw0[:], float(EPS), None, ALU.add), "v")
        vflush(e)
    v_st3 = T.n["v"]

    @block.scalar
    def _(e):
        T.wait(e, "v", v_st3)
        T.inc(e.activation(rw2[:], rw1[:], AF.Ln), "s")
        e.drain()
        T.inc(e.activation(rho[:], rw2[:], AF.Exp, scale=np.float32(-0.5)), "s")
        sflush(e)
    s_rho3 = T.n["s"]

    @block.vector
    def _(e):
        T.wait(e, "s", s_rho3)
        T.inc(e.tensor_tensor(rw2[:], rho[:], rho[:], ALU.mult), "v")
        e.drain()
        T.inc(e.tensor_tensor(rw2[:], rw1[:], rw2[:], ALU.mult), "v")
        e.drain()
        T.inc(e.tensor_scalar(rw2[:], rw2[:], -0.5, 1.5, ALU.mult, ALU.add), "v")
        e.drain()
        T.inc(e.tensor_tensor(rho[:], rho[:], rw2[:], ALU.mult), "v")
        vflush(e)
    v_rho3 = T.n["v"]

    ptb.close()

    pp1 = contextlib.ExitStack()
    tot2 = [pp1.enter_context(nc.psum_tensor(f"tot2{i}", [128, 512], F32)) for i in range(2)]
    xnp8 = [pp1.enter_context(nc.psum_tensor(f"xnp8{i}", [128, 1024], F32)) for i in range(2)]
    vps = pp1.enter_context(nc.psum_tensor("vps", [128, D], F32))

    z3_v = []
    tap_t3 = []
    ev3 = []
    wv_t = []
    pq_v = []

    def emit_z3(tt):
        @block.vector
        def _(e, tt=tt, z3_=z3[tt % 2]):
            if tt == 0:
                T.wait(e, "v", v_rho3)
                T.wait(e, "gd", gd_wvt)
            if tt >= 2:
                T.wait(e, "t", tap_t3[tt - 2])
            T.inc(e.tensor_scalar(z3_[:], stream[:, tt * D:(tt + 1) * D],
                                  negmu[:, tt:tt + 1], rho[:, tt:tt + 1],
                                  ALU.add, ALU.mult), "v")
            vflush(e)
        z3_v.append(T.n["v"])

    def emit_tap3(tt):
        @block.tensor
        def _(e, tt=tt, z3_=z3[tt % 2], xnp8_=xnp8[tt % 2]):
            w = z3_v[tt]
            if tt >= 2:
                w = max(w, ev3[tt - 2])
            T.wait(e, "v", w)
            for c in range(NCH):
                T.inc(e.matmul(xnp8_[:, c * 128:(c + 1) * 128],
                               z3_[:, c * 128:(c + 1) * 128], ids16[:],
                               start=True, stop=True), "t")
        tap_t3.append(T.n["t"])

    emit_z3(0)
    emit_tap3(0)
    for tt in range(NT):
        if tt + 1 < NT:
            emit_z3(tt + 1)

        @block.vector
        def _(e, tt=tt, xn2g_=xn2g[tt % 2], xnp8_=xnp8[tt % 2]):
            T.wait(e, "t", tap_t3[tt])
            if tt >= 2:
                T.wait(e, "t", wv_t[tt - 2])
            for c in range(NCH):
                T.inc(e.tensor_copy(xn2g_[:, c * 128:(c + 1) * 128],
                                    xnp8_[:, c * 128:(c + 1) * 128]), "v")
            vflush(e)
        ev3.append(T.n["v"])

        if tt + 1 < NT:
            emit_tap3(tt + 1)

        @block.tensor
        def _(e, tt=tt, xn2g_=xn2g[tt % 2]):
            w = ev3[tt]
            if tt >= 1:
                w = max(w, pq_v[tt - 1])
            T.wait(e, "v", w)
            for c in range(NCH):
                for half in range(2):
                    T.inc(e.matmul(vps[:, half * 512:(half + 1) * 512],
                                   xn2g_[:, c * 128:(c + 1) * 128],
                                   slotB[:, c * D + half * 512: c * D + (half + 1) * 512],
                                   start=(c == 0), stop=(c == NCH - 1),
                                   skip_group_check=True), "t")
        wv_t.append(T.n["t"])

        @block.vector
        def _(e, tt=tt, t_v=wv_t[tt]):
            T.wait(e, "t", t_v)
            T.inc(e.tensor_tensor(p_all[:, tt * D:(tt + 1) * D], vps[:], cosB[:], ALU.mult), "v")
            T.inc(e.tensor_tensor(q_all[:, tt * D:(tt + 1) * D], vps[:], sinB[:], ALU.mult), "v")
            vflush(e)
        pq_v.append(T.n["v"])

        for tx in ([tt - 1] if tt >= 1 else []) + ([tt] if tt == NT - 1 else []):
            @block.tensor
            def _(e, tx=tx):
                T.wait(e, "v", pq_v[tx])
                for half in range(2):
                    T.inc(e.matmul(tot2[half][0:16, :], ohs[:, tx * 16:(tx + 1) * 16],
                                   p_all[:, tx * D + half * 512: tx * D + (half + 1) * 512],
                                   start=(tx == 0), stop=(tx == NT - 1),
                                   skip_group_check=True), "t")
                    T.inc(e.matmul(tot2[half][32:48, :], ohs[:, tx * 16:(tx + 1) * 16],
                                   q_all[:, tx * D + half * 512: tx * D + (half + 1) * 512],
                                   start=(tx == 0), stop=(tx == NT - 1),
                                   skip_group_check=True), "t")
    t_tot = T.n["t"]

    @block.vector
    def _(e):
        T.wait(e, "t", t_tot)
        T.inc(e.tensor_copy(carw[0:16, 0:512], tot2[0][0:16, :]), "v")
        T.inc(e.tensor_copy(carw[0:16, 512:1024], tot2[1][0:16, :]), "v")
        T.inc(e.tensor_copy(carw[32:48, 0:512], tot2[0][32:48, :]), "v")
        T.inc(e.tensor_copy(carw[32:48, 512:1024], tot2[1][32:48, :]), "v")
        vflush(e)
    v_tt = T.n["v"]

    pp1.close()
    pca = contextlib.ExitStack()
    pFp = pca.enter_context(nc.psum_tensor("pFp", [128, 512], F32))
    pFq = pca.enter_context(nc.psum_tensor("pFq", [128, 512], F32))
    carp = pca.enter_context(nc.psum_tensor("carp", [16, 1024], F32))
    cari = pca.enter_context(nc.psum_tensor("cari", [16, 1024], F32))

    @block.tensor
    def _(e):
        T.wait(e, "v", v_tt)
        for c in range(NCH):
            T.inc(e.matmul(pFp[:, c * 16:(c + 1) * 16], carw[0:16, c * 128:(c + 1) * 128],
                           idsf[0:16, 0:16], start=True, stop=True), "t")
            T.inc(e.matmul(pFq[:, c * 16:(c + 1) * 16], carw[32:48, c * 128:(c + 1) * 128],
                           idsf[32:48, 32:48], start=True, stop=True), "t")
    t_tf = T.n["t"]

    seg = lambda ap_, a, b: ap_.rearrange("p (c g) -> p c g", g=16)[:, :, a:b]

    @block.vector
    def _(e):
        T.wait(e, "t", t_tf)
        T.inc(e.tensor_copy(totFp[:], pFp[:, 0:128]), "v")
        T.inc(e.tensor_copy(totFq[:], pFq[:, 0:128]), "v")
        e.drain()
        T.inc(e.tensor_tensor(sca, totFp[:], ugc[:], ALU.mult), "v")
        T.inc(e.tensor_tensor(scb, totFq[:], ugs[:], ALU.mult), "v")
        T.inc(e.tensor_tensor(care, totFp[:], ugs[:], ALU.mult), "v")
        T.inc(e.tensor_tensor(caim, totFq[:], ugc[:], ALU.mult), "v")
        e.drain()
        T.inc(e.tensor_tensor(sca, sca, scb, ALU.subtract), "v")
        T.inc(e.tensor_tensor(scb, care, caim, ALU.add), "v")
        e.drain()
        T.inc(e.tensor_copy(w15[:, 0:8], sca.rearrange("p (c g) -> p c g", g=16)[:, :, 15:16]), "v")
        T.inc(e.tensor_copy(w15[:, 8:16], scb.rearrange("p (c g) -> p c g", g=16)[:, :, 15:16]), "v")
        T.inc(e.memset(totFp[:], 0.0), "v")
        T.inc(e.memset(totFq[:], 0.0), "v")
        e.drain()
        T.inc(e.tensor_copy(seg(totFp[:], 1, 16), seg(sca, 0, 15)), "v")
        T.inc(e.tensor_copy(seg(totFq[:], 1, 16), seg(scb, 0, 15)), "v")
        e.drain()
        cur = (totFp[:], totFq[:])
        nxt = (sca, scb)
        for k in (1, 2, 4, 8):
            for ci in range(2):
                T.inc(e.tensor_copy(nxt[ci], cur[ci]), "v")
            e.drain()
            for ci in range(2):
                T.inc(e.tensor_tensor(seg(nxt[ci], k, 16), seg(cur[ci], k, 16),
                                      seg(cur[ci], 0, 16 - k), ALU.add), "v")
            e.drain()
            cur, nxt = nxt, cur
        s15r = cur[0].rearrange("p (c g) -> p c g", g=16)[:, :, 15:16]
        s15i = cur[1].rearrange("p (c g) -> p c g", g=16)[:, :, 15:16]
        T.inc(e.tensor_copy(sendb[:, 0:8], s15r), "v")
        T.inc(e.tensor_copy(sendb[:, 8:16], s15i), "v")
        e.drain()
        T.inc(e.tensor_tensor(sendb[:, 0:8], sendb[:, 0:8], w15[:, 0:8], ALU.add), "v")
        T.inc(e.tensor_tensor(sendb[:, 8:16], sendb[:, 8:16], w15[:, 8:16], ALU.add), "v")
        e.drain()
        T.inc(e.tensor_tensor(c0buf[:, 0:8], sendb[:, 0:8], u16c[:], ALU.mult), "v")
        T.inc(e.tensor_tensor(c0buf[:, 8:16], sendb[:, 8:16], u16s[:], ALU.mult), "v")
        T.inc(e.tensor_tensor(msend[:, 0:8], sendb[:, 8:16], u16c[:], ALU.mult), "v")
        T.inc(e.tensor_tensor(msend[:, 8:16], sendb[:, 0:8], u16s[:], ALU.mult), "v")
        e.drain()
        T.inc(e.tensor_tensor(sendb[:, 0:8], c0buf[:, 0:8], c0buf[:, 8:16], ALU.add), "v")
        T.inc(e.tensor_tensor(sendb[:, 8:16], msend[:, 0:8], msend[:, 8:16], ALU.subtract), "v")
        e.drain()
        for b in range(4):
            T.inc(e.tensor_scalar(msend[:, b * 16:(b + 1) * 16], sendb[:],
                                  smasks[:, b:b + 1], None, ALU.mult), "v")
        vflush(e)
    v_send = T.n["v"]

    @block.gpsimd
    def _(e):
        T.wait(e, "v", v_send)
        T.inc(e.dma_start(out=ccc_in.ap(), in_=msend[:]), "gd", 16)
        T.wait(e, "gd")
        T.inc(e.collective_compute("AllReduce", ALU.add,
                                   replica_groups=[list(range(8))],
                                   ins=[ccc_in.ap().opt()],
                                   outs=[ccc_out.ap().opt()]), "cc")
        T.wait(e, "cc")
        T.inc(e.dma_start(out=rbuf[:], in_=ccc_out.ap()), "gd", 16)
    gd_car = T.n["gd"]

    @block.vector
    def _(e):
        T.wait(e, "gd", gd_car)
        T.inc(e.tensor_scalar(c0buf[:], rbuf[:, 0:16], rmasks[:, 0:1], None, ALU.mult), "v")
        for b in range(1, 4):
            T.inc(e.tensor_scalar(msend[:, 0:16], rbuf[:, b * 16:(b + 1) * 16],
                                  rmasks[:, b:b + 1], None, ALU.mult), "v")
            e.drain()
            T.inc(e.tensor_tensor(c0buf[:], c0buf[:], msend[:, 0:16], ALU.add), "v")
            e.drain()
        for c in range(NCH):
            T.inc(e.tensor_scalar(totFp[:, c * 16:(c + 1) * 16], totFp[:, c * 16:(c + 1) * 16],
                                  c0buf[:, c:c + 1], None, ALU.add), "v")
            T.inc(e.tensor_scalar(totFq[:, c * 16:(c + 1) * 16], totFq[:, c * 16:(c + 1) * 16],
                                  c0buf[:, 8 + c:9 + c], None, ALU.add), "v")
        e.drain()
        T.inc(e.tensor_tensor(care, totFp[:], ugc[:], ALU.mult), "v")
        T.inc(e.tensor_tensor(sca, totFq[:], ugs[:], ALU.mult), "v")
        T.inc(e.tensor_tensor(caim, totFq[:], ugc[:], ALU.mult), "v")
        T.inc(e.tensor_tensor(scb, totFp[:], ugs[:], ALU.mult), "v")
        e.drain()
        T.inc(e.tensor_tensor(care, care, sca, ALU.add), "v")
        T.inc(e.tensor_tensor(caim, caim, scb, ALU.subtract), "v")
        vflush(e)
    v_car = T.n["v"]

    @block.tensor
    def _(e):
        T.wait(e, "v", v_car)
        for c in range(NCH):
            T.inc(e.matmul(carp[:, c * 128:(c + 1) * 128], care[:, c * 16:(c + 1) * 16],
                           idsf[:], start=True, stop=True), "t")
            T.inc(e.matmul(cari[:, c * 128:(c + 1) * 128], caim[:, c * 16:(c + 1) * 16],
                           idsf[:], start=True, stop=True), "t")
    t_car = T.n["t"]

    @block.vector
    def _(e):
        T.wait(e, "t", t_car)
        T.inc(e.tensor_copy(retf[0:16, :], carp[:]), "v")
        T.inc(e.tensor_copy(retf[32:48, :], cari[:]), "v")
        vflush(e)
    v_carb = T.n["v"]

    @block.gpsimd
    def _(e):
        T.wait(e, "v", v_carb)
        T.wait(e, "t", t_tot)
        T.inc(e.dma_start(out=p_all[0:1, 0:NT * D], in_=retf[0:16, :],
                          accum_op=ALU.add, allow_slow_non_contiguous=True), "gd", 16)
        T.inc(e.dma_start(out=q_all[0:1, 0:NT * D], in_=retf[32:48, :],
                          accum_op=ALU.add, allow_slow_non_contiguous=True), "gd", 16)
    gd_acc = T.n["gd"]

    pca.close()

    pp2 = contextlib.ExitStack()
    ptp = pp2.enter_context(nc.psum_tensor("ptp", [128, D], F32))
    qtp = pp2.enter_context(nc.psum_tensor("qtp", [128, D], F32))
    wop = [pp2.enter_context(nc.psum_tensor(f"wop{i}", [128, D], F32)) for i in range(2)]

    @block.sync
    def _(e):
        T.wait(e, "t", t_tot)
        T.inc(e.dma_start(out=slotB[:], in_=tern_sp.ap()), "sd", 16)
    sd_tern2 = T.n["sd"]

    ret_v = []
    ep_v = []
    for g in range(NT):
        @block.tensor
        def _(e, g=g):
            if g == 0:
                T.wait(e, "gd", gd_acc)
                T.wait(e, "gd", gd_wot)
            if len(ret_v) >= 1:
                T.wait(e, "v", ret_v[-1])
            for c in range(NCH):
                T.inc(e.matmul(ptp[:, c * 128:(c + 1) * 128],
                               p_all[:, g * D + c * 128: g * D + (c + 1) * 128],
                               tris[:], start=True, stop=True), "t")
                T.inc(e.matmul(qtp[:, c * 128:(c + 1) * 128],
                               q_all[:, g * D + c * 128: g * D + (c + 1) * 128],
                               tris[:], start=True, stop=True), "t")
        t_tri = T.n["t"]

        @block.vector
        def _(e, g=g, t_tri=t_tri):
            T.wait(e, "t", t_tri)
            for half in range(2):
                hs = slice(half * 512, (half + 1) * 512)
                T.inc(e.tensor_tensor(rs1[:], ptp[:, hs], cosT[:, hs], ALU.mult), "v")
                T.inc(e.tensor_tensor(retf[:, hs], qtp[:, hs], sinT[:, hs], ALU.mult), "v")
                T.inc(e.tensor_tensor(retf[:, hs], retf[:, hs], rs1[:], ALU.add), "v")
            vflush(e)
        ret_v.append(T.n["v"])

        @block.tensor
        def _(e, g=g, wop_=wop[g % 2]):
            T.wait(e, "v", ret_v[-1])
            if len(ep_v) >= 2:
                T.wait(e, "v", ep_v[-2])
            for c in range(NCH):
                for half in range(2):
                    T.inc(e.matmul(wop_[:, half * 512:(half + 1) * 512],
                                   retf[:, c * 128:(c + 1) * 128],
                                   slotA[:, c * D + half * 512: c * D + (half + 1) * 512],
                                   start=(c == 0), stop=(c == NCH - 1),
                                   skip_group_check=True), "t")
        t_wo = T.n["t"]

        @block.vector
        def _(e, g=g, t_wo=t_wo, wop_=wop[g % 2]):
            T.wait(e, "t", t_wo)
            for half in range(2):
                sl = stream[:, g * D + half * 512: g * D + (half + 1) * 512]
                T.inc(e.tensor_tensor(sl, sl, wop_[:, half * 512:(half + 1) * 512], ALU.add), "v")
            vflush(e)
        ep_v.append(T.n["v"])
    v_attn = T.n["v"]

    pp2.close()
    ph.close()

    v_ffn2, t_ffn2 = emit_ffn(1, slotB, ("v", v_attn), tern_ready=("sd", sd_tern2))

    @block.sync
    def _(e):
        T.wait(e, "v", v_ffn2)
        for tt in range(NT):
            T.inc(e.dma_start(out=y_t[tt], in_=stream[:, tt * D:(tt + 1) * D]), "sd", 16)
        T.wait(e, "sd")

    es.close()
    return nc


_CACHE = {}


def kernel(L0_attn_x, L0_attn_Wk, L0_attn_Wv, L0_attn_Wo, L0_attn_freqs,
           L0_ffn_weight, L0_ffn_gamma, L0_ffn_beta):
    x = np.ascontiguousarray(np.asarray(L0_attn_x, dtype=np.float32))
    B, Tg, _ = x.shape
    wvt = np.ascontiguousarray(np.asarray(L0_attn_Wv, np.float32).T)
    wot = np.ascontiguousarray(np.asarray(L0_attn_Wo, np.float32).T)
    wft = np.ascontiguousarray(np.asarray(L0_ffn_weight, np.float32).T)
    freqs = np.ascontiguousarray(np.asarray(L0_attn_freqs, np.float32).reshape(1, D))
    gam = np.ascontiguousarray(np.asarray(L0_ffn_gamma, np.float32).reshape(1, D))
    bet = np.ascontiguousarray(np.asarray(L0_ffn_beta, np.float32).reshape(1, D))

    identf = np.eye(128, dtype=np.float32)
    ones2d = np.ones((128, 128), np.float32)
    tri = np.triu(np.ones((128, 128), np.float32)).astype(ml_dtypes.bfloat16)
    onehot = np.zeros((128, 16 * NT), np.float32)
    for g in range(NT):
        onehot[:, g * 16 + g] = 1.0
    onehot = onehot.astype(ml_dtypes.bfloat16)
    onesrow = np.ones((1, 128), np.float32)
    iotacol = np.arange(128, dtype=np.float32).reshape(128, 1)
    jrow = np.broadcast_to(np.arange(128, dtype=np.float32), (128, 128)).copy()
    gidx = np.broadcast_to(np.arange(16, dtype=np.float32), (128, 16)).copy()

    if "nc" not in _CACHE:
        _CACHE["nc"] = build_nc()
    nc = _CACHE["nc"]

    in_maps = []
    for core in range(8):
        b, h = core // 2, core % 2
        smask = np.zeros((128, 4), np.float32)
        rmask = np.zeros((128, 4), np.float32)
        if h == 0:
            smask[:, b] = 1.0
        else:
            rmask[:, b] = 1.0
        in_maps.append({
            "x": np.ascontiguousarray(x[b, h * TLOC:(h + 1) * TLOC]),
            "wvt": wvt, "wot": wot, "wft": wft,
            "freqs": freqs, "gamma": gam, "beta": bet,
            "identf": identf, "ones2d": ones2d, "tri": tri, "onehot": onehot,
            "onesrow": onesrow, "iotacol": iotacol, "jrow": jrow, "gidx": gidx,
            "smask": smask, "rmask": rmask,
        })

    res = run_bass_kernel_spmd(nc, in_maps, core_ids=list(range(8)))
    out = np.empty((B, Tg, D), np.float32)
    for core in range(8):
        b, h = core // 2, core % 2
        out[b, h * TLOC:(h + 1) * TLOC] = res.results[core]["y"]
    return out


# version 16
# speedup vs baseline: 1.1005x; 1.1005x over previous
import contextlib
import numpy as np
import ml_dtypes
import concourse.bass as bass
import concourse.mybir as mybir
from concourse.bass_utils import run_bass_kernel_spmd

F32 = mybir.dt.float32
BF16 = mybir.dt.bfloat16
AF = mybir.ActivationFunctionType
ALU = mybir.AluOpType
AX = mybir.AxisListType

TWO_PI = 6.283185307179586
C1 = float(np.float32(np.ldexp(np.round(np.ldexp(TWO_PI, 11)), -11)))
C2 = float(np.float32(np.ldexp(np.round(np.ldexp(TWO_PI - C1, 33)), -33)))
C3 = float(TWO_PI - C1 - C2)
MAGIC = 12582912.0
EPS = 1e-5
D = 1024
TLOC = 2048
NT = 16
NCH = 8


class Trk:
    def __init__(self, nc, es):
        names = ["sd", "gd", "t", "v", "s", "g", "cc"]
        self.h = {n: es.enter_context(nc.semaphore(n)) for n in names}
        self.n = {n: 0 for n in names}

    def inc(self, instr, name, k=1):
        instr.then_inc(self.h[name], k)
        self.n[name] += k
        return instr

    def wait(self, eng, name, val=None):
        v = self.n[name] if val is None else val
        if v > 0:
            eng.wait_ge(self.h[name], v)


def build_nc():
    nc = bass.Bass("TRN2", target_bir_lowering=False)

    x_d = nc.dram_tensor("x", [TLOC, D], F32, kind="ExternalInput").ap()
    wvt_d = nc.dram_tensor("wvt", [D, D], F32, kind="ExternalInput").ap()
    wot_d = nc.dram_tensor("wot", [D, D], F32, kind="ExternalInput").ap()
    wft_d = nc.dram_tensor("wft", [D, D], F32, kind="ExternalInput").ap()
    f_d = nc.dram_tensor("freqs", [1, D], F32, kind="ExternalInput").ap()
    gam_d = nc.dram_tensor("gamma", [1, D], F32, kind="ExternalInput").ap()
    bet_d = nc.dram_tensor("beta", [1, D], F32, kind="ExternalInput").ap()
    idf_d = nc.dram_tensor("identf", [128, 128], F32, kind="ExternalInput").ap()
    on2_d = nc.dram_tensor("ones2d", [128, 128], F32, kind="ExternalInput").ap()
    tri_d = nc.dram_tensor("tri", [128, 128], BF16, kind="ExternalInput").ap()
    oh_d = nc.dram_tensor("onehot", [128, 16 * NT], BF16, kind="ExternalInput").ap()
    ones_d = nc.dram_tensor("onesrow", [1, 128], F32, kind="ExternalInput").ap()
    iota_d = nc.dram_tensor("iotacol", [128, 1], F32, kind="ExternalInput").ap()
    jrow_d = nc.dram_tensor("jrow", [128, 128], F32, kind="ExternalInput").ap()
    gidx_d = nc.dram_tensor("gidx", [128, 16], F32, kind="ExternalInput").ap()
    smask_d = nc.dram_tensor("smask", [128, 4], F32, kind="ExternalInput").ap()
    rmask_d = nc.dram_tensor("rmask", [128, 4], F32, kind="ExternalInput").ap()
    y_d = nc.dram_tensor("y", [TLOC, D], F32, kind="ExternalOutput").ap()
    ccm_in = [nc.dram_tensor(f"ccmi{i}", [128, 8], F32) for i in range(2)]
    ccm_out = [nc.dram_tensor(f"ccmo{i}", [128, 8], F32) for i in range(2)]
    ccc_in = nc.dram_tensor("cccin", [128, 64], F32)
    ccc_out = nc.dram_tensor("cccout", [128, 64], F32)
    tern_sp = nc.dram_tensor("ternspill", [128, NCH * D], BF16)

    x_t = x_d.rearrange("(t p) d -> t p d", p=128)
    y_t = y_d.rearrange("(t p) d -> t p d", p=128)
    wvt_c = wvt_d.rearrange("(c p) d -> c p d", p=128)
    wot_c = wot_d.rearrange("(c p) d -> c p d", p=128)
    wft_c = wft_d.rearrange("(c p) d -> c p d", p=128)

    es = contextlib.ExitStack()
    sb = lambda name, shape, dt=F32: es.enter_context(nc.sbuf_tensor(name, shape, dt))

    stream = sb("stream", [128, NT * D])
    slotA = sb("slotA", [128, NCH * D], BF16)
    slotB = sb("slotB", [128, NCH * D], BF16)
    idsf = sb("idsf", [128, 128])
    ones2 = sb("ones2", [128, 128])
    tris = sb("tris", [128, 128], BF16)
    ohs = sb("ohs", [128, 16 * NT], BF16)
    onesr = sb("onesr", [1, 128])
    iotac = sb("iotac", [128, 1])
    gidxs = sb("gidxs", [128, 16])
    smasks = sb("smasks", [128, 4])
    rmasks = sb("rmasks", [128, 4])
    gcols = sb("gcols", [128, NCH])
    bcols = sb("bcols", [128, NCH])
    fcols = sb("fcols", [128, NCH])
    sums = sb("sums", [128, NT])
    sumsq = sb("sumsq", [128, NT])
    negmu = sb("negmu", [128, NT])
    rho = sb("rho", [128, NT])
    rw0 = sb("rw0", [128, NT])
    rw1 = sb("rw1", [128, NT])
    rw2 = sb("rw2", [128, NT])
    maxc = sb("maxc", [128, 1])
    maxc2 = sb("maxc2", [128, 1])
    maxcols = sb("maxcols", [128, 8])
    srow = sb("srow", [1, 128])
    gcol = sb("gcol", [128, 1])
    sc = sb("sc", [1, 16])
    invs_col = sb("invs_col", [128, 1])
    sa_col = sb("sa_col", [128, 1])
    dcol = sb("dcol", [128, 1])
    ccmax = sb("ccmax", [1, 8])
    scol = sb("scol", [128, 8])
    acol = sb("acol", [128, 1])
    mcol = sb("mcol", [128, 1])
    angk = sb("angk", [128, D])
    vshared = sb("vshared", [128, D])

    T = Trk(nc, es)
    block = es.enter_context(nc.Block())


    def vflush(e):
        T.inc(e.drain(), "v")

    def sflush(e):
        T.inc(e.drain(), "s")

    def vspace(e, n=8):
        T.inc(e.drain(), "v")

    def emit_rr(e, dst, src, scr, short=False):
        dr = (lambda: e.drain()) if short else (lambda: None)
        T.inc(e.tensor_scalar(scr, src, float(1.0 / TWO_PI), float(MAGIC), ALU.mult, ALU.add), "v")
        dr()
        T.inc(e.tensor_scalar(scr, scr, float(-MAGIC), None, ALU.add), "v")
        dr()
        T.inc(e.scalar_tensor_tensor(dst, scr, float(-C1), src, ALU.mult, ALU.add), "v")
        dr()
        T.inc(e.scalar_tensor_tensor(dst, scr, float(-C2), dst, ALU.mult, ALU.add), "v")
        dr()
        T.inc(e.scalar_tensor_tensor(dst, scr, float(-C3), dst, ALU.mult, ALU.add), "v")

    def emit_wrap(e, dst, src, scr, short=False):
        dr = (lambda: e.drain()) if short else (lambda: None)
        T.inc(e.tensor_scalar(dst, src, float(np.pi / 2), None, ALU.add), "v")
        dr()
        T.inc(e.tensor_scalar(scr, dst, float(np.pi), None, ALU.is_gt), "v")
        dr()
        T.inc(e.scalar_tensor_tensor(dst, scr, float(-TWO_PI), dst, ALU.mult, ALU.add), "v")

    wstg = [sb(f"wstg{i}", [128, D]) for i in range(2)]
    ph0 = contextlib.ExitStack()
    stage = ph0.enter_context(nc.sbuf_tensor("stage", [128, NCH * D], F32))
    p0 = ph0.enter_context(nc.psum_tensor("p0", [128, 512], F32))

    @block.sync
    def _(e):
        for tt in range(NT):
            T.inc(e.dma_start(out=stream[:, tt * D:(tt + 1) * D], in_=x_t[tt]), "sd", 16)
        for c in range(NCH):
            T.inc(e.dma_start(out=stage[:, c * D:(c + 1) * D], in_=wft_c[c]), "sd", 16)
        for src, dst in [(idf_d, idsf), (on2_d, ones2), (tri_d, tris), (oh_d, ohs), (ones_d, onesr),
                         (iota_d, iotac), (gidx_d, gidxs),
                         (smask_d, smasks), (rmask_d, rmasks)]:
            T.inc(e.dma_start(out=dst[:], in_=src), "sd", 16)
        T.inc(e.dma_start(out=gcols[:], in_=gam_d.rearrange("r (c p) -> p (r c)", p=128), allow_slow_non_contiguous=True), "sd", 16)
        T.inc(e.dma_start(out=bcols[:], in_=bet_d.rearrange("r (c p) -> p (r c)", p=128), allow_slow_non_contiguous=True), "sd", 16)
        T.inc(e.dma_start(out=fcols[:], in_=f_d.rearrange("r (c p) -> p (r c)", p=128), allow_slow_non_contiguous=True), "sd", 16)
    sd_loads = T.n["sd"]

    @block.scalar
    def _(e):
        T.wait(e, "sd", sd_loads)
        for c in range(NCH):
            T.inc(e.activation(slotA[:, c * D:(c + 1) * D], stage[:, c * D:(c + 1) * D],
                               AF.Sign), "s")
        sflush(e)
    s0 = T.n["s"]

    @block.vector
    def _(e):
        T.wait(e, "sd", sd_loads)
        T.inc(e.memset(mcol[:], MAGIC), "v")
        T.inc(e.tensor_reduce(maxc[:], stage[:], AX.X, ALU.add,
                              apply_absolute_value=True), "v")
        vflush(e)
    v1 = T.n["v"]

    @block.tensor
    def _(e):
        T.wait(e, "v", v1)
        T.inc(e.matmul(p0[:, 0:1], ones2[:], maxc[:], start=True, stop=True), "t")
    t1 = T.n["t"]

    @block.vector
    def _(e):
        T.wait(e, "t", t1)
        T.inc(e.tensor_scalar(dcol[:], p0[:, 0:1], float(-0.7 / (D * D)), None, ALU.mult), "v")
        vflush(e)
    v_dcol = T.n["v"]

    ph0.close()

    state = {"tern_v": [], "mk_s": [], "mk_gd": []}

    def hook_grp(grp):
        c = grp

        @block.sync
        def _(e, c=c):
            if c >= 2:
                T.wait(e, "s", state["mk_s"][c - 2])
            T.inc(e.dma_start(out=wstg[c % 2][:], in_=wft_c[c]), "sd", 16)
        state["mk_gd"].append(T.n["sd"])

        @block.scalar
        def _(e, c=c):
            T.wait(e, "sd", state["mk_gd"][c])
            if c == 0:
                T.wait(e, "v", v_dcol)
            else:
                T.wait(e, "v", state["tern_v"][-1])
            T.inc(e.activation(angk[:], wstg[c % 2][:], AF.Abs), "s")
            e.drain()
            T.inc(e.activation(vshared[:], angk[:], AF.Sign,
                               bias=dcol[:], scale=1.0), "s")
            sflush(e)
        state["mk_s"].append(T.n["s"])

        @block.vector
        def _(e, c=c, s_m=state["mk_s"][-1]):
            T.wait(e, "s", s_m)
            T.inc(e.tensor_scalar(vshared[:], vshared[:], 0.5, 0.5, ALU.mult, ALU.add), "v")
            T.inc(e.tensor_tensor(slotA[:, c * D:(c + 1) * D],
                                  slotA[:, c * D:(c + 1) * D], vshared[:], ALU.mult), "v")
            vflush(e)
        state["tern_v"].append(T.n["v"])

    def hook_gm(pc):
        v_tern = state["tern_v"][-1]
        acc_gd = []
        acc_s = []
        for c in range(NCH):
            @block.sync
            def _(e, c=c):
                if c < 2:
                    T.wait(e, "s", state["mk_s"][-1])
                else:
                    T.wait(e, "s", acc_s[c - 2])
                T.inc(e.dma_start(out=wstg[c % 2][:], in_=wft_c[c]), "sd", 16)
            acc_gd.append(T.n["sd"])

            @block.scalar
            def _(e, c=c):
                T.wait(e, "sd", acc_gd[c])
                if c == 0:
                    T.wait(e, "v", v_tern)
                T.inc(e.activation(angk[:], wstg[c % 2][:], AF.Abs), "s")
                e.drain()
                T.inc(e.activation(vshared[:], angk[:], AF.Relu,
                                   bias=dcol[:], scale=1.0,
                                   accum_out=rw1[:, c:c + 1]), "s")
                T.inc(e.activation(vshared[:], angk[:], AF.Sign,
                                   bias=dcol[:], scale=1.0,
                                   accum_out=rw1[:, 8 + c:9 + c]), "s")
                sflush(e)
            acc_s.append(T.n["s"])
        s1 = acc_s[-1]
        state["s_acc"] = s1

        @block.vector
        def _(e, s1=s1):
            T.wait(e, "s", s1)
            T.inc(e.tensor_reduce(rw0[:, 0:1], rw1[:, 0:8], AX.X, ALU.add), "v")
            T.inc(e.tensor_reduce(rw0[:, 1:2], rw1[:, 8:16], AX.X, ALU.add), "v")
            vflush(e)
        v_red2 = T.n["v"]

        @block.tensor
        def _(e, v_red2=v_red2):
            T.wait(e, "v", v_red2)
            T.inc(e.matmul(pc[:, 256:257], ones2[:], rw0[:, 0:1], start=True, stop=True), "t")
            T.inc(e.matmul(pc[:, 257:258], ones2[:], rw0[:, 1:2], start=True, stop=True), "t")
        t_bc2 = T.n["t"]

        @block.vector
        def _(e, t_bc2=t_bc2):
            T.wait(e, "t", t_bc2)
            T.inc(e.tensor_copy(scol[:, 0:1], pc[:, 256:257]), "v")
            T.inc(e.tensor_scalar(scol[:, 2:3], pc[:, 257:258], float(D * D), 0.5,
                                  ALU.add, ALU.mult), "v")
            e.drain()
            T.inc(e.tensor_scalar(scol[:, 2:3], scol[:, 2:3], 1.0, None, ALU.max), "v")
            vflush(e)
        v_cnt = T.n["v"]

        @block.scalar
        def _(e, v_cnt=v_cnt):
            T.wait(e, "v", v_cnt)
            T.inc(e.activation(scol[:, 3:4], scol[:, 2:3], AF.Ln), "s")
            e.drain()
            T.inc(e.activation(scol[:, 4:5], scol[:, 3:4], AF.Exp, scale=np.float32(-1.0)), "s")
            sflush(e)
        s_rcnt = T.n["s"]

        @block.vector
        def _(e, s_rcnt=s_rcnt):
            T.wait(e, "s", s_rcnt)
            T.inc(e.tensor_tensor(scol[:, 5:6], scol[:, 2:3], scol[:, 4:5], ALU.mult), "v")
            e.drain()
            T.inc(e.tensor_scalar(scol[:, 5:6], scol[:, 5:6], -1.0, 2.0, ALU.mult, ALU.add), "v")
            e.drain()
            T.inc(e.tensor_tensor(scol[:, 4:5], scol[:, 4:5], scol[:, 5:6], ALU.mult), "v")
            e.drain()
            T.inc(e.tensor_tensor(scol[:, 5:6], dcol[:], scol[:, 2:3], ALU.mult), "v")
            e.drain()
            T.inc(e.tensor_tensor(scol[:, 5:6], scol[:, 0:1], scol[:, 5:6], ALU.subtract), "v")
            e.drain()
            T.inc(e.tensor_tensor(acol[:], scol[:, 5:6], scol[:, 4:5], ALU.mult), "v")
            vflush(e)

        @block.sync
        def _(e):
            T.wait(e, "v", v_tern)
            T.inc(e.dma_start(out=tern_sp.ap(), in_=slotA[:]), "sd", 16)
        state["sd_spill"] = T.n["sd"]

    def emit_ffn(idx, tern, stream_ready, tern_ready=None, evac_s=True, hooks=None):
        ph = contextlib.ExitStack()
        xnF = ph.enter_context(nc.sbuf_tensor(f"xnF{idx}", [128, NCH * TLOC], F32))
        z0a = [ph.enter_context(nc.sbuf_tensor(f"z0a{idx}{i}", [128, D], F32)) for i in range(2)]
        z0b = [ph.enter_context(nc.sbuf_tensor(f"z0b{idx}{i}", [128, D], F32)) for i in range(2)]
        intg = [ph.enter_context(nc.sbuf_tensor(f"intg{idx}{i}", [128, NCH * 256], BF16))
                for i in range(2)]
        rmg = ph.enter_context(nc.sbuf_tensor(f"rmg{idx}", [128, 256], F32))
        ps_tap = contextlib.ExitStack()
        zfw = [ps_tap.enter_context(nc.psum_tensor(f"zfw{idx}{i}", [128, 2048], F32))
               for i in range(2)]

        sem, cnt = stream_ready

        @block.scalar
        def _(e):
            T.wait(e, sem, cnt)
            for tt in range(NT):
                T.inc(e.activation(angk[:], stream[:, tt * D:(tt + 1) * D], AF.Square,
                                   accum_out=sumsq[:, tt:tt + 1]), "s")
            sflush(e)
        s_st = T.n["s"]

        @block.vector
        def _(e):
            T.wait(e, sem, cnt)
            for tt in range(NT):
                T.inc(e.tensor_scalar(vshared[:], stream[:, tt * D:(tt + 1) * D],
                                      1.0, 0.0, ALU.mult, ALU.add,
                                      accum_out=sums[:, tt:tt + 1]), "v")
            T.wait(e, "s", s_st)
            e.drain()
            T.inc(e.tensor_scalar(negmu[:], sums[:], float(-1.0 / D), None, ALU.mult), "v")
            T.inc(e.tensor_scalar(rw0[:], sumsq[:], float(1.0 / D), None, ALU.mult), "v")
            e.drain()
            T.inc(e.tensor_tensor(rw2[:], negmu[:], negmu[:], ALU.mult), "v")
            e.drain()
            T.inc(e.tensor_tensor(rw0[:], rw0[:], rw2[:], ALU.subtract), "v")
            e.drain()
            T.inc(e.tensor_scalar(rw1[:], rw0[:], float(1.0 + EPS), float(EPS * EPS),
                                  ALU.mult, ALU.add), "v")
            vflush(e)
        v_st = T.n["v"]

        @block.scalar
        def _(e):
            T.wait(e, "v", v_st)
            T.inc(e.activation(rw2[:], rw1[:], AF.Ln), "s")
            e.drain()
            T.inc(e.activation(rho[:], rw2[:], AF.Exp, scale=np.float32(-0.5)), "s")
            sflush(e)
        s_rho = T.n["s"]

        @block.vector
        def _(e):
            T.wait(e, "s", s_rho)
            T.inc(e.tensor_tensor(rw2[:], rho[:], rho[:], ALU.mult), "v")
            e.drain()
            T.inc(e.tensor_tensor(rw2[:], rw1[:], rw2[:], ALU.mult), "v")
            e.drain()
            T.inc(e.tensor_scalar(rw2[:], rw2[:], -0.5, 1.5, ALU.mult, ALU.add), "v")
            e.drain()
            T.inc(e.tensor_tensor(rho[:], rho[:], rw2[:], ALU.mult), "v")
            vflush(e)
        v_rho = T.n["v"]

        ev_sem = "s" if evac_s else "v"
        z_v = []
        tap_t = []
        ev_s = []

        def emit_zprep(grp):
            ta, tb = 2 * grp, 2 * grp + 1

            @block.vector
            def _(e, ta=ta, tb=tb, grp=grp, za_=z0a[grp % 2], zb_=z0b[grp % 2]):
                if grp == 0:
                    T.wait(e, "v", v_rho)
                if grp >= 2:
                    T.wait(e, "t", tap_t[grp - 2])
                T.inc(e.tensor_scalar(za_[:], stream[:, ta * D:(ta + 1) * D],
                                      negmu[:, ta:ta + 1], rho[:, ta:ta + 1],
                                      ALU.add, ALU.mult), "v")
                T.inc(e.tensor_scalar(zb_[:], stream[:, tb * D:(tb + 1) * D],
                                      negmu[:, tb:tb + 1], rho[:, tb:tb + 1],
                                      ALU.add, ALU.mult), "v")
                vflush(e)
            z_v.append(T.n["v"])

        emit_zprep(0)
        for grp in range(NT // 2):
            if grp + 1 < NT // 2:
                emit_zprep(grp + 1)

            @block.tensor
            def _(e, grp=grp, za_=z0a[grp % 2], zb_=z0b[grp % 2], zf_=zfw[grp % 2]):
                T.wait(e, "v", z_v[grp])
                if grp >= 2:
                    T.wait(e, ev_sem, ev_s[grp - 2])
                for c in range(NCH):
                    T.inc(e.matmul(zf_[:, c * 256: c * 256 + 128],
                                   za_[:, c * 128:(c + 1) * 128],
                                   idsf[:], start=True, stop=True), "t")
                    T.inc(e.matmul(zf_[:, c * 256 + 128: c * 256 + 256],
                                   zb_[:, c * 128:(c + 1) * 128],
                                   idsf[:], start=True, stop=True), "t")
            tap_t.append(T.n["t"])

            if evac_s:
                @block.scalar
                def _(e, grp=grp, t_tap=tap_t[-1], zf_=zfw[grp % 2]):
                    T.wait(e, "t", t_tap)
                    for c in range(NCH):
                        T.inc(e.activation(
                            xnF[:, c * TLOC + grp * 256: c * TLOC + (grp + 1) * 256],
                            zf_[:, c * 256:(c + 1) * 256], AF.Identity,
                            scale=gcols[:, c:c + 1], bias=bcols[:, c:c + 1]), "s")
                    sflush(e)
                ev_s.append(T.n["s"])
            else:
                @block.vector
                def _(e, grp=grp, t_tap=tap_t[-1], zf_=zfw[grp % 2]):
                    T.wait(e, "t", t_tap)
                    for c in range(NCH):
                        T.inc(e.tensor_scalar(
                            xnF[:, c * TLOC + grp * 256: c * TLOC + (grp + 1) * 256],
                            zf_[:, c * 256:(c + 1) * 256], gcols[:, c:c + 1],
                            bcols[:, c:c + 1], ALU.mult, ALU.add), "v")
                    vflush(e)
                ev_s.append(T.n["v"])
            if hooks:
                hooks["grp"](grp)
        s_xnf = ev_s[-1]
        ps_tap.close()
        ps_gm = contextlib.ExitStack()
        pc = ps_gm.enter_context(nc.psum_tensor(f"pc{idx}", [128, 512], F32))
        pr = ps_gm.enter_context(nc.psum_tensor(f"pr{idx}", [1, 512], F32))

        @block.vector
        def _(e):
            T.wait(e, ev_sem, s_xnf)
            for c in range(NCH):
                T.inc(e.tensor_reduce(maxcols[:, c:c + 1], xnF[:, c * TLOC:(c + 1) * TLOC],
                                      AX.X, ALU.max, apply_absolute_value=True), "v")
            vflush(e)
        v_mx = T.n["v"]

        @block.gpsimd
        def _(e):
            T.wait(e, "v", v_mx)
            T.inc(e.dma_start(out=ccm_in[idx].ap(), in_=maxcols[:]), "gd", 16)
            T.wait(e, "gd")
            T.inc(e.collective_compute("AllReduce", ALU.max,
                                       replica_groups=[list(range(8))],
                                       ins=[ccm_in[idx].ap().opt()],
                                       outs=[ccm_out[idx].ap().opt()]), "cc")
            T.wait(e, "cc")
            T.inc(e.dma_start(out=maxcols[:], in_=ccm_out[idx].ap()), "gd", 16)
        gd_mx = T.n["gd"]
        if hooks:
            hooks["gm"](pc)

        @block.vector
        def _(e):
            T.wait(e, "gd", gd_mx)
            T.inc(e.tensor_reduce(maxc[:], maxcols[:], AX.X, ALU.max), "v")
            vflush(e)
        v_mx2 = T.n["v"]

        @block.tensor
        def _(e):
            T.wait(e, "v", v_mx2)
            T.inc(e.matmul(pr[0:1, 0:128], maxc[:], idsf[:], start=True, stop=True), "t")
        t_gm1 = T.n["t"]

        @block.scalar
        def _(e):
            T.wait(e, "t", t_gm1)
            T.inc(e.activation(srow[:], pr[0:1, 0:128], AF.Identity), "s")
            sflush(e)
        s_gm1 = T.n["s"]

        @block.tensor
        def _(e):
            T.wait(e, "s", s_gm1)
            T.inc(e.matmul(pc[:, 0:128], onesr[:], srow[:], start=True, stop=True), "t")
        t_gm2 = T.n["t"]

        @block.vector
        def _(e):
            T.wait(e, "t", t_gm2)
            T.inc(e.tensor_reduce(gcol[:], pc[:, 0:128], AX.X, ALU.max), "v")
            e.drain()
            T.inc(e.tensor_scalar(scol[:, 6:7], gcol[:], 1e-8, None, ALU.max), "v")
            vflush(e)
        v_gm = T.n["v"]

        @block.scalar
        def _(e):
            T.wait(e, "v", v_gm)
            T.inc(e.activation(scol[:, 7:8], scol[:, 6:7], AF.Ln), "s")
            e.drain()
            T.inc(e.activation(invs_col[:], scol[:, 7:8], AF.Exp, scale=np.float32(-1.0)), "s")
            sflush(e)
        s_gm = T.n["s"]

        @block.vector
        def _(e):
            T.wait(e, "s", s_gm)
            T.inc(e.tensor_tensor(scol[:, 7:8], scol[:, 6:7], invs_col[:], ALU.mult), "v")
            e.drain()
            T.inc(e.tensor_scalar(scol[:, 7:8], scol[:, 7:8], -1.0, 2.0, ALU.mult, ALU.add), "v")
            e.drain()
            T.inc(e.tensor_tensor(invs_col[:], invs_col[:], scol[:, 7:8], ALU.mult), "v")
            e.drain()
            T.inc(e.tensor_scalar(invs_col[:], invs_col[:], 127.0, None, ALU.mult), "v")
            T.inc(e.tensor_tensor(sa_col[:], scol[:, 6:7], acol[:], ALU.mult), "v")
            e.drain()
            T.inc(e.tensor_scalar(sa_col[:], sa_col[:], float(0.5 / 127.0), None, ALU.mult), "v")
            vflush(e)
        v_cols = T.n["v"]
        ps_gm.close()
        ps_mm = contextlib.ExitStack()
        fp2 = [[ps_mm.enter_context(nc.psum_tensor(f"fp{idx}{i}{j}", [128, 512], F32))
                for j in range(4)] for i in range(2)]

        mm_t = []
        ep_q = []

        def emit_epi(tg):
            @block.vector
            def _(e, tg=tg, t_mm=mm_t[tg]):
                T.wait(e, "t", t_mm)
                fp = fp2[tg % 2]
                for tp in range(2):
                    tt = tg * 2 + tp
                    for half in range(2):
                        sl = stream[:, tt * D + half * 512: tt * D + (half + 1) * 512]
                        T.inc(e.scalar_tensor_tensor(sl, fp[tp * 2 + half][:], sa_col[:],
                                                     sl, ALU.mult, ALU.add), "v")
                vflush(e)
            ep_q.append(T.n["v"])

        for tg in range(NT // 2):
            ig = intg[tg % 2]

            @block.vector
            def _(e, tg=tg):
                if tg == 0:
                    T.wait(e, "v", v_cols)
                if len(mm_t) >= 2:
                    T.wait(e, "t", mm_t[-2])
                for c in range(NCH):
                    T.inc(e.tensor_scalar(rmg[:],
                                          xnF[:, c * TLOC + tg * 256: c * TLOC + (tg + 1) * 256],
                                          invs_col[:], mcol[:], ALU.mult, ALU.add), "v")
                    e.drain()
                    T.inc(e.tensor_scalar(ig[:, c * 256:(c + 1) * 256], rmg[:],
                                          float(-MAGIC), None, ALU.add), "v")
                vflush(e)
            v_int = T.n["v"]

            @block.tensor
            def _(e, tg=tg, v_int=v_int, ig=ig):
                T.wait(e, "v", v_int)
                if tg == 0 and tern_ready is not None:
                    tr = tern_ready() if callable(tern_ready) else tern_ready
                    T.wait(e, tr[0], tr[1])
                if tg >= 2:
                    T.wait(e, "v", ep_q[tg - 2])
                for c in range(NCH):
                    for tp in range(2):
                        for half in range(2):
                            T.inc(e.matmul(fp2[tg % 2][tp * 2 + half][:],
                                           ig[:, c * 256 + tp * 128: c * 256 + tp * 128 + 128],
                                           tern[:, c * D + half * 512: c * D + (half + 1) * 512],
                                           start=(c == 0), stop=(c == NCH - 1),
                                           skip_group_check=True), "t")
            mm_t.append(T.n["t"])
            if tg >= 1:
                emit_epi(tg - 1)
        emit_epi(NT // 2 - 1)
        v_done = T.n["v"]
        t_done = T.n["t"]
        ps_mm.close()
        ph.close()
        return v_done, t_done

    v_ffn1, t_ffn1 = emit_ffn(0, slotA, ("sd", sd_loads),
                              tern_ready=lambda: ("v", state["tern_v"][-1]),
                              evac_s=False, hooks={"grp": hook_grp, "gm": hook_gm})
    sd_spill = state["sd_spill"]

    ph = contextlib.ExitStack()
    cosB = ph.enter_context(nc.sbuf_tensor("cosB", [128, D], BF16))
    sinB = ph.enter_context(nc.sbuf_tensor("sinB", [128, D], BF16))
    cosT = ph.enter_context(nc.sbuf_tensor("cosT", [128, D], BF16))
    sinT = ph.enter_context(nc.sbuf_tensor("sinT", [128, D], BF16))
    ugc = ph.enter_context(nc.sbuf_tensor("ugc", [128, 128], F32))
    ugs = ph.enter_context(nc.sbuf_tensor("ugs", [128, 128], F32))
    u16c = ph.enter_context(nc.sbuf_tensor("u16c", [128, 8], F32))
    u16s = ph.enter_context(nc.sbuf_tensor("u16s", [128, 8], F32))
    p_all = ph.enter_context(nc.sbuf_tensor("p_all", [128, NT * D], BF16))
    q_all = ph.enter_context(nc.sbuf_tensor("q_all", [128, NT * D], BF16))
    z3 = [ph.enter_context(nc.sbuf_tensor(f"z3{i}", [128, D], BF16)) for i in range(2)]
    ids16 = ph.enter_context(nc.sbuf_tensor("ids16", [128, 128], BF16))
    xn2g = [ph.enter_context(nc.sbuf_tensor(f"xn2g{i}", [128, NCH * 128], BF16)) for i in range(2)]
    retf = ph.enter_context(nc.sbuf_tensor("retf", [128, D], BF16))
    rs1 = ph.enter_context(nc.sbuf_tensor("rs1", [128, 512], F32))
    care = rs1[:, 0:128]
    caim = rs1[:, 128:256]
    sca = rs1[:, 256:384]
    scb = rs1[:, 384:512]
    carw = ph.enter_context(nc.sbuf_tensor("carw", [128, D], F32))
    totFp = ph.enter_context(nc.sbuf_tensor("totFp", [128, 128], F32))
    totFq = ph.enter_context(nc.sbuf_tensor("totFq", [128, 128], F32))
    w15 = ph.enter_context(nc.sbuf_tensor("w15", [128, 16], F32))
    sendb = ph.enter_context(nc.sbuf_tensor("sendb", [128, 16], F32))
    msend = ph.enter_context(nc.sbuf_tensor("msend", [128, 64], F32))
    rbuf = ph.enter_context(nc.sbuf_tensor("rbuf", [128, 64], F32))
    c0buf = ph.enter_context(nc.sbuf_tensor("c0buf", [128, 16], F32))

    @block.gpsimd
    def _(e):
        T.wait(e, "sd", sd_loads)
        T.inc(e.dma_start(out=ids16[:], in_=idf_d), "gd", 16)
        for c in range(NCH):
            T.inc(e.dma_start(out=slotB[:, c * D:(c + 1) * D], in_=wvt_c[c]), "gd", 16)
    gd_wvt = T.n["gd"]

    jrows = wstg[0][:, 0:128]
    f_late = wstg[1][0:1, 0:1024]

    @block.gpsimd
    def _(e):
        T.wait(e, "sd", sd_spill)
        T.wait(e, "s", state["s_acc"])
        T.wait(e, "t", t_ffn1)
        T.inc(e.dma_start(out=jrows, in_=jrow_d), "gd", 16)
        T.inc(e.dma_start(out=f_late, in_=f_d), "gd", 16)
        gd_jr_ = T.n["gd"]
        for c in range(NCH):
            T.inc(e.dma_start(out=slotA[:, c * D:(c + 1) * D], in_=wot_c[c]), "gd", 16)
        state["gd_jr"] = gd_jr_
    gd_wot = T.n["gd"]

    ptb = contextlib.ExitStack()
    pB1 = ptb.enter_context(nc.psum_tensor("pB1", [128, 512], F32))

    for hc in range(2):
        @block.tensor
        def _(e, hc=hc):
            if hc == 0:
                T.wait(e, "v", v_ffn1)
                T.wait(e, "gd", state["gd_jr"])
            else:
                T.wait(e, "v", T.n["v"])
            for q in range(4):
                T.inc(e.matmul(pB1[:, q * 128:(q + 1) * 128], onesr[:],
                               f_late[0:1, hc * 512 + q * 128: hc * 512 + (q + 1) * 128],
                               start=True, stop=True), "t")

        @block.vector
        def _(e, hc=hc):
            T.wait(e, "t", T.n["t"])
            if hc == 0:
                T.wait(e, "s", state["s_acc"])
            sl = slice(hc * 512, (hc + 1) * 512)
            T.inc(e.tensor_scalar(vshared[:, sl], pB1[:, 0:512], iotac[:], None, ALU.mult), "v")
            emit_rr(e, vshared[:, sl], vshared[:, sl], angk[:, sl])
            emit_wrap(e, angk[:, sl], vshared[:, sl], carw[:, sl])
            vflush(e)
    v_bth = T.n["v"]

    @block.scalar
    def _(e):
        T.wait(e, "v", v_bth)
        T.inc(e.activation(sinB[:], vshared[:], AF.Sin), "s")
        T.inc(e.activation(cosB[:], angk[:], AF.Sin), "s")
        sflush(e)
    s_btab = T.n["s"]

    @block.vector
    def _(e):
        T.wait(e, "s", s_btab)
        T.wait(e, "gd", state["gd_jr"])
        for c in range(NCH):
            T.inc(e.tensor_scalar(vshared[:, c * 128:(c + 1) * 128], jrows,
                                  fcols[:, c:c + 1], None, ALU.mult), "v")
        emit_rr(e, vshared[:], vshared[:], angk[:])
        emit_wrap(e, angk[:], vshared[:], carw[:])
        vflush(e)
    v_tth = T.n["v"]

    @block.scalar
    def _(e):
        T.wait(e, "v", v_tth)
        T.inc(e.activation(sinT[:], vshared[:], AF.Sin), "s")
        T.inc(e.activation(cosT[:], angk[:], AF.Sin), "s")
        sflush(e)
    s_ttab = T.n["s"]

    @block.vector
    def _(e):
        T.wait(e, "s", s_ttab)
        T.inc(e.tensor_scalar(vshared[:, 0:8], fcols[:], 128.0, None, ALU.mult), "v")
        e.drain()
        emit_rr(e, vshared[:, 0:8], vshared[:, 0:8], angk[:, 0:8], short=True)
        e.drain()
        for c in range(NCH):
            T.inc(e.tensor_scalar(carw[:, c * 16:(c + 1) * 16], gidxs[:],
                                  vshared[:, c:c + 1], None, ALU.mult), "v")
        e.drain()
        emit_rr(e, carw[:, 0:128], carw[:, 0:128], angk[:, 0:128], short=True)
        T.inc(e.tensor_scalar(vshared[:, 8:16], vshared[:, 0:8], 16.0, None, ALU.mult), "v")
        e.drain()
        emit_rr(e, vshared[:, 8:16], vshared[:, 8:16], angk[:, 0:8], short=True)
        e.drain()
        emit_wrap(e, angk[:, 0:128], carw[:, 0:128], angk[:, 128:256], short=True)
        e.drain()
        emit_wrap(e, vshared[:, 16:24], vshared[:, 8:16], angk[:, 256:264], short=True)
        vflush(e)
    v_uth = T.n["v"]

    @block.scalar
    def _(e):
        T.wait(e, "v", v_uth)
        T.inc(e.activation(ugs[:], carw[:, 0:128], AF.Sin), "s")
        T.inc(e.activation(ugc[:], angk[:, 0:128], AF.Sin), "s")
        T.inc(e.activation(u16s[:], vshared[:, 8:16], AF.Sin), "s")
        T.inc(e.activation(u16c[:], vshared[:, 16:24], AF.Sin), "s")
        sflush(e)
    s_utab = T.n["s"]

    @block.scalar
    def _(e):
        T.wait(e, "s", s_utab)
        for tt in range(NT):
            T.inc(e.activation(angk[:], stream[:, tt * D:(tt + 1) * D], AF.Square,
                               accum_out=sumsq[:, tt:tt + 1]), "s")
        sflush(e)
    s_st3 = T.n["s"]

    @block.vector
    def _(e):
        T.wait(e, "s", s_utab)
        for tt in range(NT):
            T.inc(e.tensor_scalar(vshared[:], stream[:, tt * D:(tt + 1) * D],
                                  1.0, 0.0, ALU.mult, ALU.add,
                                  accum_out=sums[:, tt:tt + 1]), "v")
        T.wait(e, "s", s_st3)
        e.drain()
        T.inc(e.tensor_scalar(negmu[:], sums[:], float(-1.0 / D), None, ALU.mult), "v")
        T.inc(e.tensor_scalar(rw0[:], sumsq[:], float(1.0 / D), None, ALU.mult), "v")
        e.drain()
        T.inc(e.tensor_tensor(rw1[:], negmu[:], negmu[:], ALU.mult), "v")
        e.drain()
        T.inc(e.tensor_tensor(rw0[:], rw0[:], rw1[:], ALU.subtract), "v")
        e.drain()
        T.inc(e.tensor_scalar(rw1[:], rw0[:], float(EPS), None, ALU.add), "v")
        vflush(e)
    v_st3 = T.n["v"]

    @block.scalar
    def _(e):
        T.wait(e, "v", v_st3)
        T.inc(e.activation(rw2[:], rw1[:], AF.Ln), "s")
        e.drain()
        T.inc(e.activation(rho[:], rw2[:], AF.Exp, scale=np.float32(-0.5)), "s")
        sflush(e)
    s_rho3 = T.n["s"]

    @block.vector
    def _(e):
        T.wait(e, "s", s_rho3)
        T.inc(e.tensor_tensor(rw2[:], rho[:], rho[:], ALU.mult), "v")
        e.drain()
        T.inc(e.tensor_tensor(rw2[:], rw1[:], rw2[:], ALU.mult), "v")
        e.drain()
        T.inc(e.tensor_scalar(rw2[:], rw2[:], -0.5, 1.5, ALU.mult, ALU.add), "v")
        e.drain()
        T.inc(e.tensor_tensor(rho[:], rho[:], rw2[:], ALU.mult), "v")
        vflush(e)
    v_rho3 = T.n["v"]

    ptb.close()

    pp1 = contextlib.ExitStack()
    tot2 = [pp1.enter_context(nc.psum_tensor(f"tot2{i}", [128, 512], F32)) for i in range(2)]
    vps = [pp1.enter_context(nc.psum_tensor(f"vps{i}", [128, D], F32)) for i in range(2)]

    z3_v = []
    trp_sd = []
    wv_t = []
    pq_v = []

    def emit_z3(tt):
        @block.vector
        def _(e, tt=tt, z3_=z3[tt % 2]):
            if tt == 0:
                T.wait(e, "v", v_rho3)
                T.wait(e, "gd", gd_wvt)
            if tt >= 2:
                T.wait(e, "sd", trp_sd[tt - 2])
            T.inc(e.tensor_scalar(z3_[:], stream[:, tt * D:(tt + 1) * D],
                                  negmu[:, tt:tt + 1], rho[:, tt:tt + 1],
                                  ALU.add, ALU.mult), "v")
            vflush(e)
        z3_v.append(T.n["v"])

    def emit_trp(tt):
        @block.sync
        def _(e, tt=tt, z3_=z3[tt % 2], xn2g_=xn2g[tt % 2]):
            T.wait(e, "v", z3_v[tt])
            if tt >= 2:
                T.wait(e, "t", wv_t[tt - 2])
            T.inc(e.dma_start_transpose(
                out=xn2g_[:].rearrange("p (c j) -> p c j", j=128),
                in_=z3_[:]), "sd", 16)
        trp_sd.append(T.n["sd"])

    emit_z3(0)
    emit_trp(0)
    for tt in range(NT):
        if tt + 1 < NT:
            emit_z3(tt + 1)
            emit_trp(tt + 1)

        @block.tensor
        def _(e, tt=tt, xn2g_=xn2g[tt % 2], vps_=vps[tt % 2]):
            T.wait(e, "sd", trp_sd[tt])
            if tt >= 2:
                T.wait(e, "v", pq_v[tt - 2])
            for c in range(NCH):
                for half in range(2):
                    T.inc(e.matmul(vps_[:, half * 512:(half + 1) * 512],
                                   xn2g_[:, c * 128:(c + 1) * 128],
                                   slotB[:, c * D + half * 512: c * D + (half + 1) * 512],
                                   start=(c == 0), stop=(c == NCH - 1),
                                   skip_group_check=True), "t")
        wv_t.append(T.n["t"])

        @block.vector
        def _(e, tt=tt, t_v=wv_t[tt], vps_=vps[tt % 2]):
            T.wait(e, "t", t_v)
            T.inc(e.tensor_tensor(p_all[:, tt * D:(tt + 1) * D], vps_[:], cosB[:], ALU.mult), "v")
            T.inc(e.tensor_tensor(q_all[:, tt * D:(tt + 1) * D], vps_[:], sinB[:], ALU.mult), "v")
            vflush(e)
        pq_v.append(T.n["v"])

        for tx in ([tt - 1] if tt >= 1 else []) + ([tt] if tt == NT - 1 else []):
            @block.tensor
            def _(e, tx=tx):
                T.wait(e, "v", pq_v[tx])
                for half in range(2):
                    T.inc(e.matmul(tot2[half][0:16, :], ohs[:, tx * 16:(tx + 1) * 16],
                                   p_all[:, tx * D + half * 512: tx * D + (half + 1) * 512],
                                   start=(tx == 0), stop=(tx == NT - 1),
                                   skip_group_check=True), "t")
                    T.inc(e.matmul(tot2[half][32:48, :], ohs[:, tx * 16:(tx + 1) * 16],
                                   q_all[:, tx * D + half * 512: tx * D + (half + 1) * 512],
                                   start=(tx == 0), stop=(tx == NT - 1),
                                   skip_group_check=True), "t")
    t_tot = T.n["t"]

    @block.vector
    def _(e):
        T.wait(e, "t", t_tot)
        T.inc(e.tensor_copy(carw[0:16, 0:512], tot2[0][0:16, :]), "v")
        T.inc(e.tensor_copy(carw[0:16, 512:1024], tot2[1][0:16, :]), "v")
        T.inc(e.tensor_copy(carw[32:48, 0:512], tot2[0][32:48, :]), "v")
        T.inc(e.tensor_copy(carw[32:48, 512:1024], tot2[1][32:48, :]), "v")
        vflush(e)
    v_tt = T.n["v"]

    pp1.close()
    pca = contextlib.ExitStack()
    pFp = pca.enter_context(nc.psum_tensor("pFp", [128, 512], F32))
    pFq = pca.enter_context(nc.psum_tensor("pFq", [128, 512], F32))
    carp = pca.enter_context(nc.psum_tensor("carp", [16, 1024], F32))
    cari = pca.enter_context(nc.psum_tensor("cari", [16, 1024], F32))

    @block.tensor
    def _(e):
        T.wait(e, "v", v_tt)
        for c in range(NCH):
            T.inc(e.matmul(pFp[:, c * 16:(c + 1) * 16], carw[0:16, c * 128:(c + 1) * 128],
                           idsf[0:16, 0:16], start=True, stop=True), "t")
            T.inc(e.matmul(pFq[:, c * 16:(c + 1) * 16], carw[32:48, c * 128:(c + 1) * 128],
                           idsf[32:48, 32:48], start=True, stop=True), "t")
    t_tf = T.n["t"]

    seg = lambda ap_, a, b: ap_.rearrange("p (c g) -> p c g", g=16)[:, :, a:b]

    @block.vector
    def _(e):
        T.wait(e, "t", t_tf)
        T.inc(e.tensor_copy(totFp[:], pFp[:, 0:128]), "v")
        T.inc(e.tensor_copy(totFq[:], pFq[:, 0:128]), "v")
        e.drain()
        T.inc(e.tensor_tensor(sca, totFp[:], ugc[:], ALU.mult), "v")
        T.inc(e.tensor_tensor(scb, totFq[:], ugs[:], ALU.mult), "v")
        T.inc(e.tensor_tensor(care, totFp[:], ugs[:], ALU.mult), "v")
        T.inc(e.tensor_tensor(caim, totFq[:], ugc[:], ALU.mult), "v")
        e.drain()
        T.inc(e.tensor_tensor(sca, sca, scb, ALU.subtract), "v")
        T.inc(e.tensor_tensor(scb, care, caim, ALU.add), "v")
        e.drain()
        T.inc(e.tensor_copy(w15[:, 0:8], sca.rearrange("p (c g) -> p c g", g=16)[:, :, 15:16]), "v")
        T.inc(e.tensor_copy(w15[:, 8:16], scb.rearrange("p (c g) -> p c g", g=16)[:, :, 15:16]), "v")
        T.inc(e.memset(totFp[:], 0.0), "v")
        T.inc(e.memset(totFq[:], 0.0), "v")
        e.drain()
        T.inc(e.tensor_copy(seg(totFp[:], 1, 16), seg(sca, 0, 15)), "v")
        T.inc(e.tensor_copy(seg(totFq[:], 1, 16), seg(scb, 0, 15)), "v")
        e.drain()
        cur = (totFp[:], totFq[:])
        nxt = (sca, scb)
        for k in (1, 2, 4, 8):
            for ci in range(2):
                T.inc(e.tensor_copy(nxt[ci], cur[ci]), "v")
            e.drain()
            for ci in range(2):
                T.inc(e.tensor_tensor(seg(nxt[ci], k, 16), seg(cur[ci], k, 16),
                                      seg(cur[ci], 0, 16 - k), ALU.add), "v")
            e.drain()
            cur, nxt = nxt, cur
        s15r = cur[0].rearrange("p (c g) -> p c g", g=16)[:, :, 15:16]
        s15i = cur[1].rearrange("p (c g) -> p c g", g=16)[:, :, 15:16]
        T.inc(e.tensor_copy(sendb[:, 0:8], s15r), "v")
        T.inc(e.tensor_copy(sendb[:, 8:16], s15i), "v")
        e.drain()
        T.inc(e.tensor_tensor(sendb[:, 0:8], sendb[:, 0:8], w15[:, 0:8], ALU.add), "v")
        T.inc(e.tensor_tensor(sendb[:, 8:16], sendb[:, 8:16], w15[:, 8:16], ALU.add), "v")
        e.drain()
        T.inc(e.tensor_tensor(c0buf[:, 0:8], sendb[:, 0:8], u16c[:], ALU.mult), "v")
        T.inc(e.tensor_tensor(c0buf[:, 8:16], sendb[:, 8:16], u16s[:], ALU.mult), "v")
        T.inc(e.tensor_tensor(msend[:, 0:8], sendb[:, 8:16], u16c[:], ALU.mult), "v")
        T.inc(e.tensor_tensor(msend[:, 8:16], sendb[:, 0:8], u16s[:], ALU.mult), "v")
        e.drain()
        T.inc(e.tensor_tensor(sendb[:, 0:8], c0buf[:, 0:8], c0buf[:, 8:16], ALU.add), "v")
        T.inc(e.tensor_tensor(sendb[:, 8:16], msend[:, 0:8], msend[:, 8:16], ALU.subtract), "v")
        e.drain()
        for b in range(4):
            T.inc(e.tensor_scalar(msend[:, b * 16:(b + 1) * 16], sendb[:],
                                  smasks[:, b:b + 1], None, ALU.mult), "v")
        vflush(e)
    v_send = T.n["v"]

    @block.gpsimd
    def _(e):
        T.wait(e, "v", v_send)
        T.inc(e.dma_start(out=ccc_in.ap(), in_=msend[:]), "gd", 16)
        T.wait(e, "gd")
        T.inc(e.collective_compute("AllReduce", ALU.add,
                                   replica_groups=[list(range(8))],
                                   ins=[ccc_in.ap().opt()],
                                   outs=[ccc_out.ap().opt()]), "cc")
        T.wait(e, "cc")
        T.inc(e.dma_start(out=rbuf[:], in_=ccc_out.ap()), "gd", 16)
    gd_car = T.n["gd"]

    @block.vector
    def _(e):
        T.wait(e, "gd", gd_car)
        T.inc(e.tensor_scalar(c0buf[:], rbuf[:, 0:16], rmasks[:, 0:1], None, ALU.mult), "v")
        for b in range(1, 4):
            T.inc(e.tensor_scalar(msend[:, 0:16], rbuf[:, b * 16:(b + 1) * 16],
                                  rmasks[:, b:b + 1], None, ALU.mult), "v")
            e.drain()
            T.inc(e.tensor_tensor(c0buf[:], c0buf[:], msend[:, 0:16], ALU.add), "v")
            e.drain()
        for c in range(NCH):
            T.inc(e.tensor_scalar(totFp[:, c * 16:(c + 1) * 16], totFp[:, c * 16:(c + 1) * 16],
                                  c0buf[:, c:c + 1], None, ALU.add), "v")
            T.inc(e.tensor_scalar(totFq[:, c * 16:(c + 1) * 16], totFq[:, c * 16:(c + 1) * 16],
                                  c0buf[:, 8 + c:9 + c], None, ALU.add), "v")
        e.drain()
        T.inc(e.tensor_tensor(care, totFp[:], ugc[:], ALU.mult), "v")
        T.inc(e.tensor_tensor(sca, totFq[:], ugs[:], ALU.mult), "v")
        T.inc(e.tensor_tensor(caim, totFq[:], ugc[:], ALU.mult), "v")
        T.inc(e.tensor_tensor(scb, totFp[:], ugs[:], ALU.mult), "v")
        e.drain()
        T.inc(e.tensor_tensor(care, care, sca, ALU.add), "v")
        T.inc(e.tensor_tensor(caim, caim, scb, ALU.subtract), "v")
        vflush(e)
    v_car = T.n["v"]

    @block.tensor
    def _(e):
        T.wait(e, "v", v_car)
        for c in range(NCH):
            T.inc(e.matmul(carp[:, c * 128:(c + 1) * 128], care[:, c * 16:(c + 1) * 16],
                           idsf[:], start=True, stop=True), "t")
            T.inc(e.matmul(cari[:, c * 128:(c + 1) * 128], caim[:, c * 16:(c + 1) * 16],
                           idsf[:], start=True, stop=True), "t")
    t_car = T.n["t"]

    @block.vector
    def _(e):
        T.wait(e, "t", t_car)
        T.inc(e.tensor_copy(retf[0:16, :], carp[:]), "v")
        T.inc(e.tensor_copy(retf[32:48, :], cari[:]), "v")
        vflush(e)
    v_carb = T.n["v"]

    @block.gpsimd
    def _(e):
        T.wait(e, "v", v_carb)
        T.wait(e, "t", t_tot)
        T.inc(e.dma_start(out=p_all[0:1, 0:NT * D], in_=retf[0:16, :],
                          accum_op=ALU.add, allow_slow_non_contiguous=True), "gd", 16)
        T.inc(e.dma_start(out=q_all[0:1, 0:NT * D], in_=retf[32:48, :],
                          accum_op=ALU.add, allow_slow_non_contiguous=True), "gd", 16)
    gd_acc = T.n["gd"]

    pca.close()

    pp2 = contextlib.ExitStack()
    ptp = pp2.enter_context(nc.psum_tensor("ptp", [128, D], F32))
    qtp = pp2.enter_context(nc.psum_tensor("qtp", [128, D], F32))
    wop = [pp2.enter_context(nc.psum_tensor(f"wop{i}", [128, D], F32)) for i in range(2)]

    @block.sync
    def _(e):
        T.wait(e, "t", t_tot)
        T.inc(e.dma_start(out=slotB[:], in_=tern_sp.ap()), "sd", 16)
    sd_tern2 = T.n["sd"]

    ret_v = []
    ep_v = []
    for g in range(NT):
        @block.tensor
        def _(e, g=g):
            if g == 0:
                T.wait(e, "gd", gd_acc)
                T.wait(e, "gd", gd_wot)
            if len(ret_v) >= 1:
                T.wait(e, "v", ret_v[-1])
            for c in range(NCH):
                T.inc(e.matmul(ptp[:, c * 128:(c + 1) * 128],
                               p_all[:, g * D + c * 128: g * D + (c + 1) * 128],
                               tris[:], start=True, stop=True), "t")
                T.inc(e.matmul(qtp[:, c * 128:(c + 1) * 128],
                               q_all[:, g * D + c * 128: g * D + (c + 1) * 128],
                               tris[:], start=True, stop=True), "t")
        t_tri = T.n["t"]

        @block.vector
        def _(e, g=g, t_tri=t_tri):
            T.wait(e, "t", t_tri)
            for half in range(2):
                hs = slice(half * 512, (half + 1) * 512)
                T.inc(e.tensor_tensor(rs1[:], ptp[:, hs], cosT[:, hs], ALU.mult), "v")
                T.inc(e.tensor_tensor(retf[:, hs], qtp[:, hs], sinT[:, hs], ALU.mult), "v")
                T.inc(e.tensor_tensor(retf[:, hs], retf[:, hs], rs1[:], ALU.add), "v")
            vflush(e)
        ret_v.append(T.n["v"])

        @block.tensor
        def _(e, g=g, wop_=wop[g % 2]):
            T.wait(e, "v", ret_v[-1])
            if len(ep_v) >= 2:
                T.wait(e, "v", ep_v[-2])
            for c in range(NCH):
                for half in range(2):
                    T.inc(e.matmul(wop_[:, half * 512:(half + 1) * 512],
                                   retf[:, c * 128:(c + 1) * 128],
                                   slotA[:, c * D + half * 512: c * D + (half + 1) * 512],
                                   start=(c == 0), stop=(c == NCH - 1),
                                   skip_group_check=True), "t")
        t_wo = T.n["t"]

        @block.vector
        def _(e, g=g, t_wo=t_wo, wop_=wop[g % 2]):
            T.wait(e, "t", t_wo)
            for half in range(2):
                sl = stream[:, g * D + half * 512: g * D + (half + 1) * 512]
                T.inc(e.tensor_tensor(sl, sl, wop_[:, half * 512:(half + 1) * 512], ALU.add), "v")
            vflush(e)
        ep_v.append(T.n["v"])
    v_attn = T.n["v"]

    pp2.close()
    ph.close()

    v_ffn2, t_ffn2 = emit_ffn(1, slotB, ("v", v_attn), tern_ready=("sd", sd_tern2))

    @block.sync
    def _(e):
        T.wait(e, "v", v_ffn2)
        for tt in range(NT):
            T.inc(e.dma_start(out=y_t[tt], in_=stream[:, tt * D:(tt + 1) * D]), "sd", 16)
        T.wait(e, "sd")

    es.close()
    return nc


_CACHE = {}


def kernel(L0_attn_x, L0_attn_Wk, L0_attn_Wv, L0_attn_Wo, L0_attn_freqs,
           L0_ffn_weight, L0_ffn_gamma, L0_ffn_beta):
    x = np.ascontiguousarray(np.asarray(L0_attn_x, dtype=np.float32))
    B, Tg, _ = x.shape
    wvt = np.ascontiguousarray(np.asarray(L0_attn_Wv, np.float32).T)
    wot = np.ascontiguousarray(np.asarray(L0_attn_Wo, np.float32).T)
    wft = np.ascontiguousarray(np.asarray(L0_ffn_weight, np.float32).T)
    freqs = np.ascontiguousarray(np.asarray(L0_attn_freqs, np.float32).reshape(1, D))
    gam = np.ascontiguousarray(np.asarray(L0_ffn_gamma, np.float32).reshape(1, D))
    bet = np.ascontiguousarray(np.asarray(L0_ffn_beta, np.float32).reshape(1, D))

    identf = np.eye(128, dtype=np.float32)
    ones2d = np.ones((128, 128), np.float32)
    tri = np.triu(np.ones((128, 128), np.float32)).astype(ml_dtypes.bfloat16)
    onehot = np.zeros((128, 16 * NT), np.float32)
    for g in range(NT):
        onehot[:, g * 16 + g] = 1.0
    onehot = onehot.astype(ml_dtypes.bfloat16)
    onesrow = np.ones((1, 128), np.float32)
    iotacol = np.arange(128, dtype=np.float32).reshape(128, 1)
    jrow = np.broadcast_to(np.arange(128, dtype=np.float32), (128, 128)).copy()
    gidx = np.broadcast_to(np.arange(16, dtype=np.float32), (128, 16)).copy()

    if "nc" not in _CACHE:
        _CACHE["nc"] = build_nc()
    nc = _CACHE["nc"]

    in_maps = []
    for core in range(8):
        b, h = core // 2, core % 2
        smask = np.zeros((128, 4), np.float32)
        rmask = np.zeros((128, 4), np.float32)
        if h == 0:
            smask[:, b] = 1.0
        else:
            rmask[:, b] = 1.0
        in_maps.append({
            "x": np.ascontiguousarray(x[b, h * TLOC:(h + 1) * TLOC]),
            "wvt": wvt, "wot": wot, "wft": wft,
            "freqs": freqs, "gamma": gam, "beta": bet,
            "identf": identf, "ones2d": ones2d, "tri": tri, "onehot": onehot,
            "onesrow": onesrow, "iotacol": iotacol, "jrow": jrow, "gidx": gidx,
            "smask": smask, "rmask": rmask,
        })

    res = run_bass_kernel_spmd(nc, in_maps, core_ids=list(range(8)))
    out = np.empty((B, Tg, D), np.float32)
    for core in range(8):
        b, h = core // 2, core % 2
        out[b, h * TLOC:(h + 1) * TLOC] = res.results[core]["y"]
    return out
